# version 20
# baseline (speedup 1.0000x reference)
import contextlib
import math
import numpy as np
import concourse.bass as bass
import concourse.mybir as mybir
from concourse.bass_utils import run_bass_kernel_spmd

F32 = mybir.dt.float32
BF16 = mybir.dt.bfloat16
AF = mybir.ActivationFunctionType
ALU = mybir.AluOpType
AX = mybir.AxisListType

ENGINES = ("t", "v", "s", "y", "g")
DMA_QUEUES = ("y", "g", "s")


class Res:
    __slots__ = ("w", "r")

    def __init__(self):
        self.w = None
        self.r = []


class Emitter:
    def __init__(self, nc, stack, lanes=6):
        self.nc = nc
        self.q = {e: [] for e in ENGINES}
        self.esem = {e: stack.enter_context(nc.semaphore(f"dep_{e}"))
                     for e in ("t", "v", "s", "g")}
        self.ecnt = {e: 0 for e in ("t", "v", "s", "g")}
        self.lanes = lanes
        self.lsem = {}
        self.lcnt = {}
        self.lrr = {}
        for qn in DMA_QUEUES:
            self.lrr[qn] = 0
            for i in range(lanes):
                self.lsem[(qn, i)] = stack.enter_context(nc.semaphore(f"dma_{qn}{i}"))
                self.lcnt[(qn, i)] = 0
        self.known = {e: {} for e in ENGINES}
        self.fence_toks = []

    def _sem(self, key):
        return self.esem[key] if isinstance(key, str) else self.lsem[key]

    def _wait(self, eng, tok):
        key, v = tok
        if self.known[eng].get(key, 0) >= v:
            return
        sem = self._sem(key)
        self.q[eng].append(lambda E, sem=sem, v=v: E.wait_ge(sem, v))
        self.known[eng][key] = v

    def _pre(self, eng, reads, writes):
        for t in self.fence_toks:
            self._wait(eng, t)
        for res in reads:
            if res.w is not None:
                self._wait(eng, res.w)
        for res in writes:
            if res.w is not None:
                self._wait(eng, res.w)
            for t in res.r:
                self._wait(eng, t)

    def _post(self, tok, reads, writes):
        for res in reads:
            res.r.append(tok)
        for res in writes:
            res.w = tok
            res.r = []
        return tok

    def emit(self, eng, fn, reads=(), writes=()):
        self._pre(eng, reads, writes)
        sem = self.esem[eng]
        self.q[eng].append(lambda E, fn=fn, sem=sem: fn(E).then_inc(sem, 1))
        self.ecnt[eng] += 1
        return self._post((eng, self.ecnt[eng]), reads, writes)

    def emit_group(self, eng, fns, reads=(), writes=()):
        self._pre(eng, reads, writes)
        sem = self.esem[eng]
        for fn in fns[:-1]:
            self.q[eng].append(fn)
        self.q[eng].append(lambda E, fn=fns[-1], sem=sem: fn(E).then_inc(sem, 1))
        self.ecnt[eng] += 1
        return self._post((eng, self.ecnt[eng]), reads, writes)

    def dma(self, qn, fns, reads=(), writes=()):
        if callable(fns):
            fns = [fns]
        lane = self.lrr[qn]
        self.lrr[qn] = (lane + 1) % self.lanes
        key = (qn, lane)
        sem = self.lsem[key]
        prev = self.lcnt[key]
        if prev > 0:
            self._wait(qn, (key, prev))
        self._pre(qn, reads, writes)
        for fn in fns:
            self.q[qn].append(lambda E, fn=fn, sem=sem: fn(E).then_inc(sem, 16))
        self.lcnt[key] = prev + 16 * len(fns)
        return self._post((key, self.lcnt[key]), reads, writes)

    def wait_tokens(self, eng, toks):
        for t in toks:
            if t is not None:
                self._wait(eng, t)

    def fence(self):
        toks = [(e, c) for e, c in self.ecnt.items() if c > 0]
        toks += [(k, c) for k, c in self.lcnt.items() if c > 0]
        self.fence_toks = toks
        self.known = {e: dict(self.known[e]) for e in ENGINES}

    def emit_into_blocks(self, block):
        for name, attr in (("g", "gpsimd"), ("y", "sync"), ("t", "tensor"),
                           ("v", "vector"), ("s", "scalar")):
            closures = self.q[name]

            def run(E, closures=closures):
                for c in closures:
                    c(E)
            getattr(block, attr)(run)


H = 2048
NH = 16
HD = 128
IC = 16
NEXP = 8
M = 512
MS = 1024
SCL = 1.0 / math.sqrt(HD)


def build_nc(S=4096, debug=False):
    OWN = S // 2
    nQT = OWN // 128
    nTT = S // 128
    nKT = S // 128
    nc = bass.Bass("TRN2", target_bir_lowering=False, debug=False)

    d = lambda n, shp, dt, kind: nc.dram_tensor(n, shp, dt, kind=kind).ap()
    x_seq = d("x_seq", [S, H], F32, "ExternalInput")
    x_own = d("x_own", [OWN, H], F32, "ExternalInput")
    wqkv_t = d("wqkv_t", [48, 16, 128, 128], F32, "ExternalInput")
    bqkv_pc = d("bqkv_pc", [128, 48], F32, "ExternalInput")
    wout_t = d("wout_t", [16, 128, H], F32, "ExternalInput")
    bout_row = d("bout_row", [1, H], F32, "ExternalInput")
    gate_wt = d("gate_wt", [16, 128, NEXP], F32, "ExternalInput")
    wg = d("wg", [NEXP, 128, 8192], F32, "ExternalInput")
    wu = d("wu", [NEXP, 128, 8192], F32, "ExternalInput")
    wd = d("wd", [NEXP, 128, 8192], F32, "ExternalInput")
    swg = d("swg", [2, 128, 8192], F32, "ExternalInput")
    swu = d("swu", [2, 128, 8192], F32, "ExternalInput")
    swd = d("swd", [2, 128, 8192], F32, "ExternalInput")
    mask_in = d("mask", [128, 256], F32, "ExternalInput")
    maskt_in = d("maskt", [128, 256], F32, "ExternalInput")
    ident_in = d("ident", [128, 128], F32, "ExternalInput")
    out = d("out", [OWN, H], F32, "ExternalOutput")

    SK = "ExternalOutput" if debug else "Internal"
    kf_s = d("kf_s", [NH, 128, S], BF16, SK)
    vf_s = d("vf_s", [NH, 128, S], BF16, SK)
    qf_s = d("qf_s", [NH, 128, OWN], BF16, SK)
    h_s = d("h_s", [nQT, 128, H], F32, SK)
    hn_s = d("hn_s", [nQT, 128, H], BF16, SK)
    attn_dbg = d("attn_dbg", [128, NH * (S // 2)], BF16, SK) if debug else None
    p_dbg = d("p_dbg", [4, 128, S], BF16, SK) if debug else None
    vt_dbg = d("vt_dbg", [128, (S // 128) * 144], BF16, SK) if debug else None
    av_dbg = d("av_dbg", [2, 128, 129], F32, SK) if debug else None

    with contextlib.ExitStack() as top:
        E_ = top.enter_context
        sb = lambda n, shp, dt: E_(nc.sbuf_tensor(n, shp, dt))
        em = Emitter(nc, top)

        psA = E_(nc.psum_tensor("psA", [128, 1024], F32))
        psB = E_(nc.psum_tensor("psB", [128, 1024], F32))
        psC = E_(nc.psum_tensor("psC", [128, 512], F32))
        psD = E_(nc.psum_tensor("psD", [128, 512], F32))
        psT2 = E_(nc.psum_tensor("psT2", [128, 1024], BF16))
        psT = E_(nc.psum_tensor("psT", [128, 1024], BF16))

        bqkv_sb = sb("bqkv_sb", [128, 48], F32)
        sbq_sb = sb("sbq_sb", [128, 16], F32)
        mask_sb = sb("mask_sb", [128, 256], F32)
        eps_sb = sb("eps_sb", [128, 1], F32)
        ones1 = sb("ones1", [1, 128], BF16)
        r_bqkv, r_sbq, r_mask, r_eps = Res(), Res(), Res(), Res()
        r_ones = Res()
        em.dma("y", lambda E: E.dma_start(out=bqkv_sb[:], in_=bqkv_pc), writes=[r_bqkv])
        em.dma("y", lambda E: E.dma_start(out=mask_sb[:], in_=mask_in), writes=[r_mask])
        em.emit("v", lambda E: E.memset(eps_sb[:], 1e-5), writes=[r_eps])
        em.emit("v", lambda E: E.memset(ones1[:], 1.0), writes=[r_ones])
        em.emit("v", lambda E: E.tensor_scalar_mul(sbq_sb[:], bqkv_sb[:, 0:16], SCL),
                reads=[r_bqkv], writes=[r_sbq])
        ident_bf = sb("ident_bf", [128, 128], BF16)
        ident512 = sb("ident512", [128, 512], BF16)
        maskt_sb = sb("maskt_sb", [128, 256], F32)
        r_ident, r_id512, r_maskt = Res(), Res(), Res()
        em.dma("g", lambda E: E.dma_start(out=ident_bf[:], in_=ident_in), writes=[r_ident])
        em.dma("y", lambda E: E.dma_start(out=maskt_sb[:], in_=maskt_in), writes=[r_maskt])
        for j4 in range(4):
            em.emit("v", lambda E, j4=j4: E.tensor_copy(ident512[:, j4 * 128:(j4 + 1) * 128],
                                                        ident_bf[:]),
                    reads=[r_ident], writes=[r_id512])

        r_psT = Res()
        r_psT2 = Res()
        tp_rr = [0]

        def emit_tpose(srcs, dsts, reads, writes, batch_dsts=None):
            n = len(srcs)
            for bi, b0 in enumerate(range(0, n, 8)):
                blk = list(range(b0, min(b0 + 8, n)))
                k = len(blk)
                tb = psT if tp_rr[0] % 2 == 0 else psT2
                r_tb = r_psT if tp_rr[0] % 2 == 0 else r_psT2
                tp_rr[0] += 1
                em.emit_group("t", [
                    (lambda E, j=j, i=i, tb=tb: E.transpose(tb[:, i * 128:(i + 1) * 128], srcs[j],
                                                            ident_bf[:]))
                    for i, j in enumerate(blk)
                ], reads=list(reads) + [r_ident], writes=[r_tb])
                if batch_dsts is not None:
                    em.emit("s", lambda E, bi=bi, k=k, tb=tb: E.activation(
                        batch_dsts[bi],
                        tb[:, 0:k * 128].rearrange("p (i s) -> p i s", i=k),
                        AF.Identity),
                        reads=[r_tb], writes=list(writes))
                else:
                    for i, j in enumerate(blk):
                        em.emit("s", lambda E, j=j, i=i, tb=tb: E.activation(
                            dsts[j], tb[:, i * 128:(i + 1) * 128], AF.Identity),
                            reads=[r_tb], writes=list(writes))

        def emit_ln(xa, r_xa, xn, r_xn, smt, r_sm, bnb, r_bnb):
            for i in range(4):
                em.emit("v", lambda E, i=i: E.bn_stats(bnb[:, i * 6:(i + 1) * 6],
                                                       xa[:, i * 512:(i + 1) * 512]),
                        reads=[r_xa], writes=[r_bnb])
            em.emit("v", lambda E: E.bn_aggr(smt[:, 0:2], bnb[:]), reads=[r_bnb], writes=[r_sm])
            em.emit("s", lambda E: E.activation(smt[:, 2:3], smt[:, 1:2], AF.Ln,
                                                bias=eps_sb[:], scale=1.0),
                    reads=[r_sm, r_eps], writes=[r_sm])
            em.emit("s", lambda E: E.activation(smt[:, 3:4], smt[:, 2:3], AF.Exp,
                                                bias=0.0, scale=-0.5),
                    reads=[r_sm], writes=[r_sm])
            em.emit("v", lambda E: E.tensor_scalar(xn[:], xa[:], smt[:, 0:1], smt[:, 3:4],
                                                   ALU.subtract, ALU.mult),
                    reads=[r_xa, r_sm], writes=[r_xn])

        p12 = contextlib.ExitStack()
        sb12 = lambda n, shp, dt: p12.enter_context(nc.sbuf_tensor(n, shp, dt))
        xnow = sb12("xnow", [128, IC * OWN], BF16)
        xt_12 = [sb12(f"xt12_{k}", [128, H], F32) for k in range(2)]
        xn_12 = [sb12(f"xn12_{k}", [128, H], BF16) for k in range(2)]
        sm_12 = [sb12(f"sm12_{k}", [128, 8], F32) for k in range(2)]
        bnb_12 = [sb12(f"bnb12_{k}", [128, 24], F32) for k in range(2)]

        r_xt = [Res(), Res()]
        r_xn = [Res(), Res()]
        r_sm = [Res(), Res()]
        r_bnb = [Res(), Res()]
        r_xnow = [Res() for _ in range(nQT)]
        r_qfs = [Res() for _ in range(NH)]
        r_kfs = [Res() for _ in range(NH)]
        r_vfs = [Res() for _ in range(NH)]

        def ln_tile12(t):
            k = t % 2
            em.dma("y", lambda E, t=t, k=k: E.dma_start(out=xt_12[k][:],
                                                        in_=x_own[t * 128:(t + 1) * 128, :]),
                   writes=[r_xt[k]])
            emit_ln(xt_12[k], r_xt[k], xn_12[k], r_xn[k], sm_12[k], r_sm[k], bnb_12[k], r_bnb[k])
            xv = xnow[:].rearrange("p (i s) -> p i s", i=IC)
            emit_tpose([xn_12[k][:, i * 128:(i + 1) * 128] for i in range(IC)], None,
                       [r_xn[k]], [r_xnow[t]],
                       batch_dsts=[xv[:, 0:8, t * 128:(t + 1) * 128],
                                   xv[:, 8:16, t * 128:(t + 1) * 128]])

        for t in range(4):
            ln_tile12(t)
        nOQ = OWN // 512
        ps4 = [psA[:, 0:512], psA[:, 512:1024], psB[:, 0:512], psB[:, 512:1024]]
        r_ps4 = [Res() for _ in range(4)]
        qst = [sb12(f"qst{j}", [128, 512], BF16) for j in range(4)]
        r_qst = [Res() for _ in range(4)]
        wq8 = [sb12(f"wq8_{j}", [128, 2048], BF16) for j in range(16)]
        r_wq8 = [Res() for _ in range(16)]
        for o in range(16):
            em.dma("g", [
                (lambda E, o=o, i=i: E.dma_start(
                    out=wq8[o][:, i * 128:(i + 1) * 128], in_=wqkv_t[o, i]))
                for i in range(IC)
            ], writes=[r_wq8[o]])
        for n in range(nOQ):
            for t in range(4 + 4 * (n - 1), 4 + 4 * n):
                if 0 <= t < nQT:
                    ln_tile12(t)
            rdt = [r_xnow[tt] for tt in range(n * 4, n * 4 + 4)]
            for o in range(16):
                pj = (n * 16 + o) % 4
                em.emit_group("t", [
                    (lambda E, i=i, o=o, n=n, pj=pj: E.matmul(
                        ps4[pj], wq8[o][:, i * 128:(i + 1) * 128],
                        xnow[:, i * OWN + n * 512: i * OWN + (n + 1) * 512],
                        start=(i == 0), stop=(i == IC - 1)))
                    for i in range(IC)
                ], reads=[r_wq8[o]] + rdt, writes=[r_ps4[pj]])
                sj = (n * 16 + o) % 4
                em.emit("s", lambda E, o=o, pj=pj, sj=sj: E.activation(
                    qst[sj][:], ps4[pj], AF.Identity,
                    bias=sbq_sb[:, o:o + 1], scale=SCL),
                    reads=[r_ps4[pj], r_sbq], writes=[r_qst[sj]])
                em.dma("y", lambda E, o=o, n=n, sj=sj: E.dma_start(
                    out=qf_s[o][:, n * 512:(n + 1) * 512], in_=qst[sj][:]),
                    reads=[r_qst[sj]], writes=[r_qfs[o]])

        em.fence()
        p12.close()

        p34 = contextlib.ExitStack()
        sb34 = lambda n, shp, dt: p34.enter_context(nc.sbuf_tensor(n, shp, dt))
        xnf_r = sb34("xnf_r", [128, IC * 1024], BF16)
        xt_34 = [sb34(f"xt34_{k}", [128, H], F32) for k in range(2)]
        xn_34 = [sb34(f"xn34_{k}", [128, H], BF16) for k in range(2)]
        sm_34 = [sb34(f"sm34_{k}", [128, 8], F32) for k in range(2)]
        bnb_34 = [sb34(f"bnb34_{k}", [128, 24], F32) for k in range(2)]

        r_xt34 = [Res(), Res()]
        r_xn34 = [Res(), Res()]
        r_sm34 = [Res(), Res()]
        r_bnb34 = [Res(), Res()]
        r_xnfr = [Res() for _ in range(8)]

        def ln_tile34(t):
            k = t % 2
            em.dma("y", lambda E, t=t, k=k: E.dma_start(out=xt_34[k][:],
                                                        in_=x_seq[t * 128:(t + 1) * 128, :]),
                   writes=[r_xt34[k]])
            emit_ln(xt_34[k], r_xt34[k], xn_34[k], r_xn34[k], sm_34[k], r_sm34[k],
                    bnb_34[k], r_bnb34[k])
            xfv = xnf_r[:].rearrange("p (i s) -> p i s", i=IC)
            ts = t % 8
            emit_tpose([xn_34[k][:, i * 128:(i + 1) * 128] for i in range(IC)], None,
                       [r_xn34[k]], [r_xnfr[ts]],
                       batch_dsts=[xfv[:, 0:8, ts * 128:(ts + 1) * 128],
                                   xfv[:, 8:16, ts * 128:(ts + 1) * 128]])

        for t in range(8):
            ln_tile34(t)

        nSQ = S // 512
        kst = [sb34(f"kst{j}", [128, 512], BF16) for j in range(4)]
        r_kst = [Res() for _ in range(4)]
        wkv = [sb34(f"wkv{o}", [128, 2048], BF16) for o in range(32)]
        r_wkv = [Res() for _ in range(32)]
        for o in range(16, 48):
            em.dma("g", [
                (lambda E, o=o, i=i: E.dma_start(
                    out=wkv[o - 16][:, i * 128:(i + 1) * 128], in_=wqkv_t[o, i]))
                for i in range(IC)
            ], writes=[r_wkv[o - 16]])
        for n in range(nSQ):
            rdt = [r_xnfr[tt % 8] for tt in range(n * 4, n * 4 + 4)]
            c0 = (n % 2) * 512
            for oo in range(32):
                o = 16 + oo
                pj = (n * 32 + oo) % 4
                em.emit_group("t", [
                    (lambda E, i=i, oo=oo, c0=c0, pj=pj: E.matmul(
                        ps4[pj], wkv[oo][:, i * 128:(i + 1) * 128],
                        xnf_r[:, i * 1024 + c0: i * 1024 + c0 + 512],
                        start=(i == 0), stop=(i == IC - 1)))
                    for i in range(IC)
                ], reads=[r_wkv[oo]] + rdt, writes=[r_ps4[pj]])
                sj = (n * 32 + oo) % 4
                em.emit("s", lambda E, o=o, pj=pj, sj=sj: E.activation(
                    kst[sj][:], ps4[pj], AF.Identity,
                    bias=bqkv_sb[:, o:o + 1], scale=1.0),
                    reads=[r_ps4[pj], r_bqkv], writes=[r_kst[sj]])
                if o < 32:
                    em.dma("y", lambda E, o=o, n=n, sj=sj: E.dma_start(
                        out=kf_s[o - 16][:, n * 512:(n + 1) * 512], in_=kst[sj][:]),
                        reads=[r_kst[sj]], writes=[r_kfs[o - 16]])
                else:
                    em.dma("y", lambda E, o=o, n=n, sj=sj: E.dma_start(
                        out=vf_s[o - 32][:, n * 512:(n + 1) * 512], in_=kst[sj][:]),
                        reads=[r_kst[sj]], writes=[r_vfs[o - 32]])
            for t in range(8 + 4 * n, min(8 + 4 * n + 4, nTT)):
                ln_tile34(t)

        em.fence()
        p34.close()

        p56 = contextlib.ExitStack()
        attn_fm = p56.enter_context(nc.sbuf_tensor("attn_fm", [128, NH * OWN], BF16))
        p5 = contextlib.ExitStack()
        sb5 = lambda n, shp, dt: p5.enter_context(nc.sbuf_tensor(n, shp, dt))
        kf_sb = [sb5(f"kf_{k}", [128, S], BF16) for k in range(2)]
        qf_sb = [sb5(f"qf_{k}", [128, OWN], BF16) for k in range(2)]
        vfm_sb = [sb5(f"vfm_{k}", [128, S], BF16) for k in range(2)]
        vt_sb = [sb5(f"vt_{k}", [128, nKT * 129], BF16) for k in range(2)]
        pt_sb = [sb5(f"pt_{k}", [128, nKT * 512], BF16) for k in range(2)]
        at_sb = [sb5(f"at_{k}", [128, 128], BF16) for k in range(2)]
        sm5 = [sb5(f"sm5_{k}", [128, 8], F32) for k in range(2)]
        cmax = [sb5(f"cmax_{k}", [128, 40], F32) for k in range(2)]
        cmbf = [sb5(f"cmbf_{k}", [128, 1], BF16) for k in range(2)]
        mg_bf = [sb5(f"mg_{k}", [1, 1], BF16) for k in range(2)]
        bias_ap = [sb5(f"biasap_{k}", [128, 1], F32) for k in range(2)]

        r_kf = [Res(), Res()]
        r_qf = [Res(), Res()]
        r_vfm = [Res(), Res()]
        r_vt = [Res(), Res()]
        r_pt = [Res(), Res()]
        r_at = [Res(), Res()]
        r_sm5 = [Res(), Res()]
        r_cmax = [Res(), Res()]
        r_cmbf = [Res(), Res()]
        r_mg = [Res(), Res()]
        r_bias = [Res(), Res()]
        r_st4 = [Res() for _ in range(4)]
        r_av = [Res(), Res()]
        r_afm = {}
        st4 = [psA[:, 0:512], psA[:, 512:1024], psB[:, 0:512], psB[:, 512:1024]]

        for k in range(2):
            em.emit_group("v", [
                (lambda E, k=k, kt=kt: E.memset(vt_sb[k][:, kt * 129 + 128: kt * 129 + 129], 1.0))
                for kt in range(nKT)
            ], writes=[r_vt[k]])

        def load_head(h):
            hb = h % 2
            em.dma("y", lambda E, h=h, hb=hb: E.dma_start(out=kf_sb[hb][:], in_=kf_s[h]),
                   reads=[r_kfs[h]], writes=[r_kf[hb]])
            em.dma("y", lambda E, h=h, hb=hb: E.dma_start(out=qf_sb[hb][:], in_=qf_s[h]),
                   reads=[r_qfs[h]], writes=[r_qf[hb]])
            em.dma("g", lambda E, h=h, hb=hb: E.dma_start(out=vfm_sb[hb][:], in_=vf_s[h]),
                   reads=[r_vfs[h]], writes=[r_vfm[hb]])
            vv = vt_sb[hb][:].rearrange("p (k c) -> p k c", c=129)
            emit_tpose([vfm_sb[hb][:, kt * 128:(kt + 1) * 128] for kt in range(nKT)], None,
                       [r_vfm[hb]], [r_vt[hb]],
                       batch_dsts=[vv[:, b0:min(b0 + 8, nKT), 0:128]
                                   for b0 in range(0, nKT, 8)])

        load_head(0)
        load_head(1)

        nQG = nQT // 4
        slot_ctr = [0]

        def st_block(h, g, gb, kt, do_stats):
            hb = h % 2
            jmin = max(0, kt // 2 - 4 * g)
            N = 512 - jmin * 128
            sj = slot_ctr[0] % 4
            slot_ctr[0] += 1
            stb = st4[sj]
            r_st = r_st4[sj]
            em.emit("t", lambda E, kt=kt, g=g, hb=hb, jmin=jmin, N=N, stb=stb: E.matmul(
                stb[:, 0:N], kf_sb[hb][:, kt * 128:(kt + 1) * 128],
                qf_sb[hb][:, (4 * g + jmin) * 128:(4 * g + 4) * 128],
                start=True, stop=True),
                reads=[r_kf[hb], r_qf[hb]], writes=[r_st])
            if do_stats:
                em.emit("v", lambda E, N=N, kt=kt, stb=stb, gb=gb: E.reduce_max(
                    cmax[gb][:, kt:kt + 1], stb[:, 0:N], axis=AX.X),
                    reads=[r_st], writes=[r_cmax[gb]])
            else:
                if kt >= 8 * g:
                    mh = 0 if kt % 2 == 0 else 1
                    em.emit("v", lambda E, stb=stb, mh=mh: E.tensor_add(
                        stb[:, 0:128], stb[:, 0:128], maskt_sb[:, mh * 128:(mh + 1) * 128]),
                        reads=[r_maskt], writes=[r_st])
                em.emit("s", lambda E, kt=kt, N=N, stb=stb, gb=gb: E.activation(
                    pt_sb[gb][:, kt * 512: kt * 512 + N], stb[:, 0:N], AF.Exp,
                    bias=bias_ap[gb][:], scale=1.0),
                    reads=[r_st, r_bias[gb]], writes=[r_pt[gb]])

        def st_mms(h, g, gb, do_stats, inter=None):
            nktm = 8 * g + 8
            pk = 0
            pn = 8 * inter[1] + 8 if inter is not None else 0
            for kt in range(nktm):
                st_block(h, g, gb, kt, do_stats)
                if inter is not None and pk < pn:
                    st_block(inter[0], inter[1], inter[2], pk, False)
                    pk += 1
            while inter is not None and pk < pn:
                st_block(inter[0], inter[1], inter[2], pk, False)
                pk += 1

        def m_chain(h, g, gb):
            nktm = 8 * g + 8
            em.emit("v", lambda E, gb=gb, nktm=nktm: E.reduce_max(
                sm5[gb][:, 0:1], cmax[gb][:, 0:nktm], axis=AX.X),
                reads=[r_cmax[gb]], writes=[r_sm5[gb]])
            em.emit("v", lambda E, gb=gb: E.tensor_copy(cmbf[gb][:], sm5[gb][:, 0:1]),
                    reads=[r_sm5[gb]], writes=[r_cmbf[gb]])
            em.emit("t", lambda E, gb=gb: E.matmul(psC[0:1, 256:384], cmbf[gb][:], ident_bf[:],
                                                   start=True, stop=True),
                    reads=[r_cmbf[gb], r_ident], writes=[r_av[0]])
            em.emit("v", lambda E, gb=gb: E.reduce_max(sm5[gb][0:1, 2:3], psC[0:1, 256:384], axis=AX.X),
                    reads=[r_av[0]], writes=[r_sm5[gb]])
            em.emit("v", lambda E, gb=gb: E.tensor_copy(mg_bf[gb][:], sm5[gb][0:1, 2:3]),
                    reads=[r_sm5[gb]], writes=[r_mg[gb]])
            em.emit("t", lambda E, gb=gb: E.matmul(psC[:, 384:385], ones1[:], mg_bf[gb][:],
                                                   start=True, stop=True),
                    reads=[r_ones, r_mg[gb]], writes=[r_av[0]])
            em.emit("v", lambda E, gb=gb: E.tensor_scalar_mul(bias_ap[gb][:], psC[:, 384:385], -1.0),
                    reads=[r_av[0]], writes=[r_bias[gb]])

        def av_evac(h, g, gb, j):
            qt = 4 * g + j
            ab = j % 2
            av = psC[:, 0:129] if ab == 0 else psD[:, 0:129]
            r_a = r_av[ab]
            em.emit("v", lambda E, gb=gb, av=av, ab=ab: E.tensor_scalar_max(
                sm5[gb][:, 4 + ab:5 + ab], av[:, 128:129], 1e-35),
                reads=[r_a], writes=[r_sm5[gb]])
            em.emit("v", lambda E, gb=gb, ab=ab: E.reciprocal(
                sm5[gb][:, 6 + ab:7 + ab], sm5[gb][:, 4 + ab:5 + ab]),
                reads=[r_sm5[gb]], writes=[r_sm5[gb]])
            em.emit("v", lambda E, gb=gb, av=av, ab=ab: E.tensor_scalar_mul(
                at_sb[ab][:], av[:, 0:128], sm5[gb][:, 6 + ab:7 + ab]),
                reads=[r_a, r_sm5[gb]], writes=[r_at[ab]])
            r_afm[(h, qt)] = Res()
            emit_tpose([at_sb[ab][:]],
                       [attn_fm[:, h * OWN + qt * 128: h * OWN + (qt + 1) * 128]],
                       [r_at[ab]], [r_afm[(h, qt)]])

        def av_group(h, g, gb):
            hb = h % 2
            for j in range(4):
                qt = 4 * g + j
                nkt = 2 * (qt + 1)
                ab = j % 2
                av = psC[:, 0:129] if ab == 0 else psD[:, 0:129]
                r_a = r_av[ab]
                fns = []
                for kt in range(nkt):
                    jmin = max(0, kt // 2 - 4 * g)
                    jr = j - jmin
                    fns.append(lambda E, kt=kt, jr=jr, gb=gb, hb=hb, av=av, nkt=nkt: E.matmul(
                        av, pt_sb[gb][:, kt * 512 + jr * 128: kt * 512 + (jr + 1) * 128],
                        vt_sb[hb][:, kt * 129: kt * 129 + 129],
                        start=(kt == 0), stop=(kt == nkt - 1)))
                em.emit_group("t", fns, reads=[r_pt[gb], r_vt[hb]], writes=[r_a])
                if j > 0:
                    av_evac(h, g, gb, j - 1)
            av_evac(h, g, gb, 3)

        prev = None
        for h in range(NH):
            for g in range(nQG):
                gb = (h * nQG + g) % 2
                st_mms(h, g, gb, do_stats=True, inter=prev)
                if prev is not None:
                    av_group(prev[0], prev[1], prev[2])
                m_chain(h, g, gb)
                prev = (h, g, gb)
            ph, pg, pgb = prev
            st_mms(ph, pg, pgb, do_stats=False)
            av_group(ph, pg, pgb)
            prev = None
            if h + 2 < NH:
                load_head(h + 2)

        if debug:
            r_all = [r_afm[(hh, qq)] for hh in range(NH) for qq in range(nQT)]
            em.dma("y", lambda E: E.dma_start(out=attn_dbg, in_=attn_fm[:]), reads=r_all)

        em.fence()
        p5.close()

        p6 = contextlib.ExitStack()
        sb6 = lambda n, shp, dt: p6.enter_context(nc.sbuf_tensor(n, shp, dt))
        wout_sb = sb6("wout_sb", [128, 16 * H], BF16)
        bout_f = sb6("bout_f", [1, H], F32)
        bout_bf = sb6("bout_bf", [1, H], BF16)
        r_boutf, r_boutbf = Res(), Res()
        em.dma("y", lambda E: E.dma_start(out=bout_f[:], in_=bout_row), writes=[r_boutf])
        em.emit("v", lambda E: E.tensor_copy(bout_bf[:], bout_f[:]),
                reads=[r_boutf], writes=[r_boutbf])
        xt_6 = [sb6(f"xt6_{k}", [128, H], F32) for k in range(2)]
        ht_6 = [sb6(f"ht6_{k}", [128, H], F32) for k in range(2)]
        hn_6 = [sb6(f"hn6_{k}", [128, H], BF16) for k in range(2)]
        sm_6 = [sb6(f"sm6_{k}", [128, 8], F32) for k in range(2)]
        bnb_6 = [sb6(f"bnb6_{k}", [128, 24], F32) for k in range(2)]

        r_wout = [Res() for _ in range(16)]
        r_xt6 = [Res(), Res()]
        r_ht6 = [Res(), Res()]
        r_hn6 = [Res(), Res()]
        r_sm6 = [Res(), Res()]
        r_bnb6 = [Res(), Res()]
        r_hs = [Res() for _ in range(nQT)]
        r_hns = [Res() for _ in range(nQT)]

        for i in range(16):
            em.dma("g", lambda E, i=i: E.dma_start(out=wout_sb[:, i * H:(i + 1) * H], in_=wout_t[i]),
                   writes=[r_wout[i]])

        for qt in range(nQT):
            k = qt % 2
            em.dma("y", lambda E, qt=qt, k=k: E.dma_start(out=xt_6[k][:],
                                                          in_=x_own[qt * 128:(qt + 1) * 128, :]),
                   writes=[r_xt6[k]])
            for oc in range(4):
                pj = (qt * 4 + oc) % 4
                fns = [lambda E, oc=oc, pj=pj: E.matmul(
                    ps4[pj], ones1[:], bout_bf[0:1, oc * 512:(oc + 1) * 512],
                    start=True, stop=False)]
                fns += [
                    (lambda E, i=i, qt=qt, oc=oc, pj=pj: E.matmul(
                        ps4[pj], attn_fm[:, i * OWN + qt * 128: i * OWN + (qt + 1) * 128],
                        wout_sb[:, i * H + oc * 512: i * H + oc * 512 + 512],
                        start=False, stop=(i == 15)))
                    for i in range(16)
                ]
                em.emit_group("t", fns,
                              reads=[r_ones, r_boutbf] +
                                    [r_afm[(i, qt)] for i in range(NH)] + r_wout,
                              writes=[r_ps4[pj]])
                em.emit("v", lambda E, oc=oc, k=k, pj=pj: E.tensor_add(
                    ht_6[k][:, oc * 512:(oc + 1) * 512], ps4[pj], xt_6[k][:, oc * 512:(oc + 1) * 512]),
                    reads=[r_ps4[pj], r_xt6[k]], writes=[r_ht6[k]])
            em.dma("y", lambda E, qt=qt, k=k: E.dma_start(out=h_s[qt], in_=ht_6[k][:]),
                   reads=[r_ht6[k]], writes=[r_hs[qt]])
            emit_ln(ht_6[k], r_ht6[k], hn_6[k], r_hn6[k], sm_6[k], r_sm6[k], bnb_6[k], r_bnb6[k])
            em.dma("y", lambda E, qt=qt, k=k: E.dma_start(out=hn_s[qt], in_=hn_6[k][:]),
                   reads=[r_hn6[k]], writes=[r_hns[qt]])

        em.fence()
        p6.close()
        p56.close()

        p7 = contextlib.ExitStack()
        sb7 = lambda n, shp, dt: p7.enter_context(nc.sbuf_tensor(n, shp, dt))
        TPP = nQT // 2
        hnf = sb7("hnf", [128, IC * TPP * 128], BF16)
        y_sb = sb7("y_sb", [128, TPP * H], F32)
        slot = [sb7(f"slot{j}", [128, 8192], BF16) for j in range(3)]
        gate_bf = sb7("gate_bf", [128, 128], BF16)
        comb = sb7("comb", [128, TPP * 8], F32)
        lgt = sb7("lgt", [128, 16], F32)
        sm7 = sb7("sm7", [128, 8], F32)
        sil_7 = [sb7(f"sil7_{k}", [128, 512], F32) for k in range(2)]
        td_7 = [sb7(f"td7_{k}", [128, 512], BF16) for k in range(2)]
        tdf_7 = [sb7(f"tdf7_{k}", [128, 512], BF16) for k in range(2)]
        tdfs = [sb7(f"tdfs_{k}", [128, TPP * 512], BF16) for k in range(2)]
        ht_7 = [sb7(f"ht7_{k}", [128, H], F32) for k in range(2)]
        hnt_7 = [sb7(f"hnt7_{k}", [128, H], BF16) for k in range(2)]
        r_hnt7 = [Res(), Res()]
        ot_7 = [sb7("ot7_0", [128, H], F32)]

        r_slot = [Res() for _ in range(3)]
        r_gate = Res()
        r_comb = Res()
        r_lgt = Res()
        r_sm7 = Res()
        r_sil = [Res(), Res()]
        r_td = [Res(), Res()]
        r_tdf = [Res(), Res()]
        r_tdfs = [Res(), Res()]
        r_hnf = [Res() for _ in range(TPP)]
        r_ht7 = [Res(), Res()]
        r_ot7 = [Res()]
        r_y = Res()
        st_toks = []

        em.dma("g", [
            (lambda E, i=i: E.dma_start(out=gate_bf[:, i * 8:(i + 1) * 8], in_=gate_wt[i]))
            for i in range(16)
        ], writes=[r_gate])

        slot_rr = [0]

        def load_slot(fns, rds=()):
            j = slot_rr[0]
            slot_rr[0] = (j + 1) % 3
            em.dma("g", [
                (lambda E, fn=fn, j=j: fn(E, slot[j])) for fn in fns
            ], reads=list(rds), writes=[r_slot[j]])
            return j

        def wg_loader(src, e):
            return [(lambda E, S_, src=src, e=e: E.dma_start(out=S_[:], in_=src[e]))]

        def wd_loader(e):
            return [(lambda E, S_, e=e: E.dma_start(out=S_[:], in_=wd[e]))]

        def sw_loader(src, half):
            return [(lambda E, S_, src=src, half=half: E.dma_start(out=S_[:], in_=src[half]))]

        def swd_loader(half):
            return [(lambda E, S_, half=half: E.dma_start(out=S_[:], in_=swd[half]))]

        y_ps = [psA[:, 0:512], psA[:, 512:1024]]
        r_yps = [Res(), Res()]
        g_ps2 = [psC, psC]
        u_ps2 = [psD, psB[:, 512:1024]]
        r_gc = Res()
        r_gps2 = [r_gc, r_gc]
        r_ups2 = [Res(), Res()]
        lg_ps = psB[:, 0:512]
        r_lgps = Res()

        for pa in range(2):
            t0 = pa * TPP
            for t in range(TPP):
                kb = t % 2
                em.dma("y", lambda E, t=t, pa=pa, kb=kb: E.dma_start(
                    out=hnt_7[kb][:], in_=hn_s[pa * TPP + t]),
                    reads=[r_hns[pa * TPP + t]], writes=[r_hnt7[kb]])
                hv = hnf[:].rearrange("p (i s) -> p i s", i=IC)
                emit_tpose([hnt_7[kb][:, i * 128:(i + 1) * 128] for i in range(IC)], None,
                           [r_hnt7[kb]], [r_hnf[t]],
                           batch_dsts=[hv[:, 0:8, t * 128:(t + 1) * 128],
                                       hv[:, 8:16, t * 128:(t + 1) * 128]])
            em.emit_group("v", [
                (lambda E, t=t: E.memset(y_sb[:, t * H:(t + 1) * H], 0.0)) for t in range(TPP)
            ], writes=[r_y])

            for t in range(TPP):
                em.emit_group("t", [
                    (lambda E, i=i, t=t: E.matmul(
                        lg_ps[:, 0:8], hnf[:, i * TPP * 128 + t * 128: i * TPP * 128 + (t + 1) * 128],
                        gate_bf[:, i * 8:(i + 1) * 8], start=(i == 0), stop=(i == IC - 1)))
                    for i in range(IC)
                ], reads=[r_gate, r_hnf[t]], writes=[r_lgps])
                em.emit("v", lambda E: E.tensor_copy(lgt[:, 0:8], lg_ps[:, 0:8]),
                        reads=[r_lgps], writes=[r_lgt])
                em.emit("v", lambda E: E.reduce_max(sm7[:, 0:1], lgt[:, 0:8], axis=AX.X),
                        reads=[r_lgt], writes=[r_sm7])
                em.emit("v", lambda E: E.tensor_scalar(lgt[:, 8:16], lgt[:, 0:8], sm7[:, 0:1],
                                                       None, ALU.is_equal),
                        reads=[r_lgt, r_sm7], writes=[r_lgt])
                em.emit("v", lambda E, t=t: E.scalar_tensor_tensor(
                    comb[:, t * 8:(t + 1) * 8], lgt[:, 8:16], -1e30, lgt[:, 0:8],
                    ALU.mult, ALU.add), reads=[r_lgt], writes=[r_comb])
                em.emit("v", lambda E, t=t: E.reduce_max(sm7[:, 1:2], comb[:, t * 8:(t + 1) * 8],
                                                         axis=AX.X),
                        reads=[r_comb], writes=[r_sm7])
                em.emit("v", lambda E: E.tensor_sub(sm7[:, 2:3], sm7[:, 1:2], sm7[:, 0:1]),
                        reads=[r_sm7], writes=[r_sm7])
                em.emit("s", lambda E: E.activation(sm7[:, 3:4], sm7[:, 2:3], AF.Exp,
                                                    bias=0.0, scale=1.0),
                        reads=[r_sm7], writes=[r_sm7])
                em.emit("v", lambda E: E.tensor_scalar(sm7[:, 4:5], sm7[:, 3:4], 1.0, None, ALU.add),
                        reads=[r_sm7], writes=[r_sm7])
                em.emit("v", lambda E: E.reciprocal(sm7[:, 5:6], sm7[:, 4:5]),
                        reads=[r_sm7], writes=[r_sm7])
                em.emit("v", lambda E: E.tensor_mul(sm7[:, 6:7], sm7[:, 3:4], sm7[:, 5:6]),
                        reads=[r_sm7], writes=[r_sm7])
                em.emit("v", lambda E, t=t: E.tensor_scalar(
                    comb[:, t * 8:(t + 1) * 8], lgt[:, 0:8], sm7[:, 1:2], sm7[:, 6:7],
                    ALU.is_equal, ALU.mult), reads=[r_lgt, r_sm7], writes=[r_comb])
                em.emit("v", lambda E, t=t: E.tensor_scalar(
                    lgt[:, 8:16], lgt[:, 0:8], sm7[:, 0:1], sm7[:, 5:6],
                    ALU.is_equal, ALU.mult), reads=[r_lgt, r_sm7], writes=[r_lgt])
                em.emit("v", lambda E, t=t: E.tensor_add(
                    comb[:, t * 8:(t + 1) * 8], comb[:, t * 8:(t + 1) * 8], lgt[:, 8:16]),
                    reads=[r_lgt], writes=[r_comb])

            def gut_part(t, e, jg, ju):
                kb = t % 2
                g_ps, u_ps = g_ps2[kb], u_ps2[kb]
                r_gps, r_ups = r_gps2[kb], r_ups2[kb]
                em.emit_group("t", [
                    (lambda E, i=i, t=t, jg=jg: E.matmul(
                        g_ps[:], hnf[:, i * TPP * 128 + t * 128: i * TPP * 128 + (t + 1) * 128],
                        slot[jg][:, i * 512:(i + 1) * 512], start=(i == 0), stop=(i == IC - 1)))
                    for i in range(IC)
                ], reads=[r_slot[jg], r_hnf[t]], writes=[r_gps])
                em.emit_group("t", [
                    (lambda E, i=i, t=t, ju=ju: E.matmul(
                        u_ps[:], hnf[:, i * TPP * 128 + t * 128: i * TPP * 128 + (t + 1) * 128],
                        slot[ju][:, i * 512:(i + 1) * 512], start=(i == 0), stop=(i == IC - 1)))
                    for i in range(IC)
                ], reads=[r_slot[ju], r_hnf[t]], writes=[r_ups])
                em.emit("s", lambda E, kb=kb: E.activation(sil_7[kb][:], g_ps[:], AF.Silu),
                        reads=[r_gps], writes=[r_sil[kb]])
                if e is not None:
                    em.emit("v", lambda E, kb=kb, t=t, e=e: E.scalar_tensor_tensor(
                        td_7[kb][:], sil_7[kb][:], comb[:, t * 8 + e: t * 8 + e + 1], u_ps[:],
                        ALU.mult, ALU.mult),
                        reads=[r_sil[kb], r_comb, r_ups], writes=[r_td[kb]])
                else:
                    em.emit("v", lambda E, kb=kb: E.tensor_mul(td_7[kb][:], sil_7[kb][:], u_ps[:]),
                            reads=[r_sil[kb], r_ups], writes=[r_td[kb]])
            def tp_part(t):
                kb = t % 2
                emit_tpose([td_7[kb][:, mc * 128:(mc + 1) * 128] for mc in range(4)], None,
                           [r_td[kb]], [r_tdf[kb]],
                           batch_dsts=[tdf_7[kb][:].rearrange("p (i s) -> p i s", i=4)])

            def y_part(t, jd):
                kb = t % 2
                for oc in range(4):
                    yj = (t * 4 + oc) % 2
                    em.emit_group("t", [
                        (lambda E, mc=mc, oc=oc, kb=kb, jd=jd, yj=yj: E.matmul(
                            y_ps[yj], tdf_7[kb][:, mc * 128:(mc + 1) * 128],
                            slot[jd][:, mc * 2048 + oc * 512: mc * 2048 + oc * 512 + 512],
                            start=(mc == 0), stop=(mc == 3)))
                        for mc in range(4)
                    ], reads=[r_tdf[kb], r_slot[jd]], writes=[r_yps[yj]])
                    em.emit("v", lambda E, t=t, oc=oc, yj=yj: E.scalar_tensor_tensor(
                        y_sb[:, t * H + oc * 512: t * H + oc * 512 + 512], y_ps[yj], 1.0,
                        y_sb[:, t * H + oc * 512: t * H + oc * 512 + 512], ALU.mult, ALU.add),
                        reads=[r_yps[yj]], writes=[r_y])

            for e in range(NEXP):
                jg = load_slot(wg_loader(wg, e))
                ju = load_slot(wg_loader(wu, e))
                jd = load_slot(wd_loader(e))
                prev = None
                for t in range(TPP):
                    gut_part(t, e, jg, ju)
                    if prev is not None:
                        y_part(prev, jd)
                    tp_part(t)
                    prev = t
                y_part(prev, jd)

            for half in range(2):
                jg = load_slot(sw_loader(swg, half))
                ju = load_slot(sw_loader(swu, half))
                tp_pend = []
                for t in range(TPP):
                    kb = t % 2
                    g_ps, u_ps = g_ps2[kb], u_ps2[kb]
                    r_gps, r_ups = r_gps2[kb], r_ups2[kb]
                    em.emit_group("t", [
                        (lambda E, i=i, t=t, jg=jg, g_ps=g_ps: E.matmul(
                            g_ps[:], hnf[:, i * TPP * 128 + t * 128: i * TPP * 128 + (t + 1) * 128],
                            slot[jg][:, i * 512:(i + 1) * 512], start=(i == 0), stop=(i == IC - 1)))
                        for i in range(IC)
                    ], reads=[r_slot[jg], r_hnf[t]], writes=[r_gps])
                    em.emit_group("t", [
                        (lambda E, i=i, t=t, ju=ju, u_ps=u_ps: E.matmul(
                            u_ps[:], hnf[:, i * TPP * 128 + t * 128: i * TPP * 128 + (t + 1) * 128],
                            slot[ju][:, i * 512:(i + 1) * 512], start=(i == 0), stop=(i == IC - 1)))
                        for i in range(IC)
                    ], reads=[r_slot[ju], r_hnf[t]], writes=[r_ups])
                    em.emit("s", lambda E, kb=kb, g_ps=g_ps: E.activation(sil_7[kb][:], g_ps[:], AF.Silu),
                            reads=[r_gps], writes=[r_sil[kb]])
                    em.emit("v", lambda E, kb=kb, u_ps=u_ps: E.tensor_mul(td_7[kb][:], sil_7[kb][:], u_ps[:]),
                            reads=[r_sil[kb], r_ups], writes=[r_td[kb]])
                    tp_pend.append((t, kb))
                    if len(tp_pend) > 1:
                        pt_, pkb = tp_pend.pop(0)
                        emit_tpose([td_7[pkb][:, mc * 128:(mc + 1) * 128] for mc in range(4)], None,
                                   [r_td[pkb]], [r_tdfs[half]],
                                   batch_dsts=[tdfs[half][:, pt_ * 512:(pt_ + 1) * 512]
                                               .rearrange("p (i s) -> p i s", i=4)])
                while tp_pend:
                    pt_, pkb = tp_pend.pop(0)
                    emit_tpose([td_7[pkb][:, mc * 128:(mc + 1) * 128] for mc in range(4)], None,
                               [r_td[pkb]], [r_tdfs[half]],
                               batch_dsts=[tdfs[half][:, pt_ * 512:(pt_ + 1) * 512]
                                           .rearrange("p (i s) -> p i s", i=4)])
            for half in range(2):
                jd = load_slot(swd_loader(half))
                for t in range(TPP):
                    for oc in range(4):
                        yj = (t * 4 + oc) % 2
                        em.emit_group("t", [
                            (lambda E, mc=mc, oc=oc, t=t, jd=jd, half=half, yj=yj: E.matmul(
                                y_ps[yj], tdfs[half][:, t * 512 + mc * 128: t * 512 + (mc + 1) * 128],
                                slot[jd][:, mc * 2048 + oc * 512: mc * 2048 + oc * 512 + 512],
                                start=(mc == 0), stop=(mc == 3)))
                            for mc in range(4)
                        ], reads=[r_tdfs[half], r_slot[jd]], writes=[r_yps[yj]])
                        em.emit("v", lambda E, t=t, oc=oc, yj=yj: E.scalar_tensor_tensor(
                            y_sb[:, t * H + oc * 512: t * H + oc * 512 + 512], y_ps[yj], 1.0,
                            y_sb[:, t * H + oc * 512: t * H + oc * 512 + 512], ALU.mult, ALU.add),
                            reads=[r_yps[yj]], writes=[r_y])

            for t in range(TPP):
                k = t % 2
                em.dma("y", lambda E, t=t, pa=pa, k=k: E.dma_start(out=ht_7[k][:],
                                                                  in_=h_s[pa * TPP + t]),
                       reads=[r_hs[pa * TPP + t]], writes=[r_ht7[k]])
                em.emit("v", lambda E, t=t, k=k: E.tensor_add(
                    ot_7[0][:], y_sb[:, t * H:(t + 1) * H], ht_7[k][:]),
                    reads=[r_y, r_ht7[k]], writes=[r_ot7[0]])
                gt = pa * TPP + t
                st_toks.append(em.dma("y", lambda E, gt=gt: E.dma_start(
                    out=out[gt * 128:(gt + 1) * 128, :], in_=ot_7[0][:]),
                    reads=[r_ot7[0]]))
            if pa == 0:
                em.fence()

        em.wait_tokens("y", st_toks)
        p7.close()

        block = E_(nc.Block())
        em.emit_into_blocks(block)
    return nc


def host_prep(X, W_qkv, b_qkv, W_out, b_out, gate_w, Wg, Wu, Wd, sWg, sWu, sWd):
    X = np.asarray(X, dtype=np.float32)
    B_, S_, H_ = X.shape
    c = np.ascontiguousarray
    wqkv_t = c(np.asarray(W_qkv, np.float32).T.reshape(16, 128, 48, 128)
               .transpose(2, 0, 1, 3))
    bqkv_pc = c(np.asarray(b_qkv, np.float32).reshape(48, 128).T)
    wout_t = c(np.asarray(W_out, np.float32).T.reshape(16, 128, 2048))
    bout_row = c(np.asarray(b_out, np.float32).reshape(1, 2048))
    gate_wt = c(np.asarray(gate_w, np.float32).T.reshape(16, 128, 8))
    wg = c(np.asarray(Wg, np.float32).reshape(8, 16, 128, 512)
           .transpose(0, 2, 1, 3).reshape(8, 128, 8192))
    wu = c(np.asarray(Wu, np.float32).reshape(8, 16, 128, 512)
           .transpose(0, 2, 1, 3).reshape(8, 128, 8192))
    wd = c(np.asarray(Wd, np.float32).reshape(8, 4, 128, 2048)
           .transpose(0, 2, 1, 3).reshape(8, 128, 8192))
    _swg_r = np.asarray(sWg, np.float32).reshape(16, 128, 1024).transpose(1, 0, 2)
    swg = c(np.stack([_swg_r[:, :, hh * 512:(hh + 1) * 512].reshape(128, 8192)
                      for hh in (0, 1)]))
    _swu_r = np.asarray(sWu, np.float32).reshape(16, 128, 1024).transpose(1, 0, 2)
    swu = c(np.stack([_swu_r[:, :, hh * 512:(hh + 1) * 512].reshape(128, 8192)
                      for hh in (0, 1)]))
    swd = c(np.asarray(sWd, np.float32).reshape(2, 4, 128, 2048)
            .transpose(0, 2, 1, 3).reshape(2, 128, 8192))

    ii = np.arange(128)[:, None]
    jj = np.arange(256)[None, :]
    masks = [c(np.where(jj <= ii + 128 * p, 0.0, -1e9).astype(np.float32)) for p in (0, 1)]
    maskts = [c(np.concatenate([m.T[0:128], m.T[128:256]], axis=1)) for m in masks]
    ident = np.eye(128, dtype=np.float32)
    own = [np.concatenate([np.arange(256 * j + 128 * p, 256 * j + 128 * p + 128)
                           for j in range(S_ // 256)]) for p in (0, 1)]

    in_maps = []
    for core in range(8):
        b, p = core // 2, core % 2
        in_maps.append(dict(
            x_seq=c(X[b % B_]), x_own=c(X[b % B_][own[p]]),
            wqkv_t=wqkv_t, bqkv_pc=bqkv_pc, wout_t=wout_t, bout_row=bout_row,
            gate_wt=gate_wt, wg=wg, wu=wu, wd=wd, swg=swg, swu=swu, swd=swd,
            mask=masks[p], maskt=maskts[p], ident=ident,
        ))
    return in_maps, own


def kernel(L0_attn_x, L0_attn_W_qkv, L0_attn_b_qkv, L0_attn_W_out, L0_attn_b_out,
           L0_ffn_gate_w, L0_ffn_Wg, L0_ffn_Wu, L0_ffn_Wd, L0_ffn_sWg, L0_ffn_sWu,
           L0_ffn_sWd):
    X = np.asarray(L0_attn_x, dtype=np.float32)
    in_maps, own = host_prep(X, L0_attn_W_qkv, L0_attn_b_qkv, L0_attn_W_out,
                             L0_attn_b_out, L0_ffn_gate_w, L0_ffn_Wg, L0_ffn_Wu,
                             L0_ffn_Wd, L0_ffn_sWg, L0_ffn_sWu, L0_ffn_sWd)
    nc = build_nc(S=X.shape[1])
    res = run_bass_kernel_spmd(nc, in_maps, core_ids=list(range(8)))
    Y = np.empty_like(X)
    for core in range(8):
        b, p = core // 2, core % 2
        Y[b][own[p]] = res.results[core]["out"]
    return Y


# version 21
# speedup vs baseline: 1.0385x; 1.0385x over previous
import contextlib
import math
import numpy as np
import concourse.bass as bass
import concourse.mybir as mybir
from concourse.bass_utils import run_bass_kernel_spmd

F32 = mybir.dt.float32
BF16 = mybir.dt.bfloat16
AF = mybir.ActivationFunctionType
ALU = mybir.AluOpType
AX = mybir.AxisListType

ENGINES = ("t", "v", "s", "y", "g")
DMA_QUEUES = ("y", "g", "s")


class Res:
    __slots__ = ("w", "r")

    def __init__(self):
        self.w = None
        self.r = []


class Emitter:
    def __init__(self, nc, stack, lanes=6):
        self.nc = nc
        self.q = {e: [] for e in ENGINES}
        self.esem = {e: stack.enter_context(nc.semaphore(f"dep_{e}"))
                     for e in ("t", "v", "s", "g")}
        self.ecnt = {e: 0 for e in ("t", "v", "s", "g")}
        self.lanes = lanes
        self.lsem = {}
        self.lcnt = {}
        self.lrr = {}
        for qn in DMA_QUEUES:
            self.lrr[qn] = 0
            for i in range(lanes):
                self.lsem[(qn, i)] = stack.enter_context(nc.semaphore(f"dma_{qn}{i}"))
                self.lcnt[(qn, i)] = 0
        self.known = {e: {} for e in ENGINES}
        self.fence_toks = []

    def _sem(self, key):
        return self.esem[key] if isinstance(key, str) else self.lsem[key]

    def _wait(self, eng, tok):
        key, v = tok
        if self.known[eng].get(key, 0) >= v:
            return
        sem = self._sem(key)
        self.q[eng].append(lambda E, sem=sem, v=v: E.wait_ge(sem, v))
        self.known[eng][key] = v

    def _pre(self, eng, reads, writes):
        for t in self.fence_toks:
            self._wait(eng, t)
        for res in reads:
            if res.w is not None:
                self._wait(eng, res.w)
        for res in writes:
            if res.w is not None:
                self._wait(eng, res.w)
            for t in res.r:
                self._wait(eng, t)

    def _post(self, tok, reads, writes):
        for res in reads:
            res.r.append(tok)
        for res in writes:
            res.w = tok
            res.r = []
        return tok

    def emit(self, eng, fn, reads=(), writes=()):
        self._pre(eng, reads, writes)
        sem = self.esem[eng]
        self.q[eng].append(lambda E, fn=fn, sem=sem: fn(E).then_inc(sem, 1))
        self.ecnt[eng] += 1
        return self._post((eng, self.ecnt[eng]), reads, writes)

    def emit_group(self, eng, fns, reads=(), writes=()):
        self._pre(eng, reads, writes)
        sem = self.esem[eng]
        for fn in fns[:-1]:
            self.q[eng].append(fn)
        self.q[eng].append(lambda E, fn=fns[-1], sem=sem: fn(E).then_inc(sem, 1))
        self.ecnt[eng] += 1
        return self._post((eng, self.ecnt[eng]), reads, writes)

    def dma(self, qn, fns, reads=(), writes=()):
        if callable(fns):
            fns = [fns]
        lane = self.lrr[qn]
        self.lrr[qn] = (lane + 1) % self.lanes
        key = (qn, lane)
        sem = self.lsem[key]
        prev = self.lcnt[key]
        if prev > 0:
            self._wait(qn, (key, prev))
        self._pre(qn, reads, writes)
        for fn in fns:
            self.q[qn].append(lambda E, fn=fn, sem=sem: fn(E).then_inc(sem, 16))
        self.lcnt[key] = prev + 16 * len(fns)
        return self._post((key, self.lcnt[key]), reads, writes)

    def wait_tokens(self, eng, toks):
        for t in toks:
            if t is not None:
                self._wait(eng, t)

    def fence(self):
        toks = [(e, c) for e, c in self.ecnt.items() if c > 0]
        toks += [(k, c) for k, c in self.lcnt.items() if c > 0]
        self.fence_toks = toks
        self.known = {e: dict(self.known[e]) for e in ENGINES}

    def emit_into_blocks(self, block):
        for name, attr in (("g", "gpsimd"), ("y", "sync"), ("t", "tensor"),
                           ("v", "vector"), ("s", "scalar")):
            closures = self.q[name]

            def run(E, closures=closures):
                for c in closures:
                    c(E)
            getattr(block, attr)(run)


H = 2048
NH = 16
HD = 128
IC = 16
NEXP = 8
M = 512
MS = 1024
SCL = 1.0 / math.sqrt(HD)


def build_nc(S=4096, debug=False):
    OWN = S // 2
    nQT = OWN // 128
    nTT = S // 128
    nKT = S // 128
    nc = bass.Bass("TRN2", target_bir_lowering=False, debug=False)

    d = lambda n, shp, dt, kind: nc.dram_tensor(n, shp, dt, kind=kind).ap()
    x_seq = d("x_seq", [S, H], F32, "ExternalInput")
    x_own = d("x_own", [OWN, H], F32, "ExternalInput")
    wqkv_t = d("wqkv_t", [48, 128, 2048], F32, "ExternalInput")
    bqkv_pc = d("bqkv_pc", [128, 48], F32, "ExternalInput")
    wout_t = d("wout_t", [16, 128, H], F32, "ExternalInput")
    bout_row = d("bout_row", [1, H], F32, "ExternalInput")
    gate_wt = d("gate_wt", [16, 128, NEXP], F32, "ExternalInput")
    wg = d("wg", [NEXP, 128, 8192], F32, "ExternalInput")
    wu = d("wu", [NEXP, 128, 8192], F32, "ExternalInput")
    wd = d("wd", [NEXP, 128, 8192], F32, "ExternalInput")
    swg = d("swg", [2, 128, 8192], F32, "ExternalInput")
    swu = d("swu", [2, 128, 8192], F32, "ExternalInput")
    swd = d("swd", [2, 128, 8192], F32, "ExternalInput")
    mask_in = d("mask", [128, 256], F32, "ExternalInput")
    maskt_in = d("maskt", [128, 256], F32, "ExternalInput")
    ident_in = d("ident", [128, 128], F32, "ExternalInput")
    out = d("out", [OWN, H], F32, "ExternalOutput")

    SK = "ExternalOutput" if debug else "Internal"
    kf_s = d("kf_s", [NH, 128, S], BF16, SK)
    vf_s = d("vf_s", [NH, 128, S], BF16, SK)
    qf_s = d("qf_s", [NH, 128, OWN], BF16, SK)
    h_s = d("h_s", [nQT, 128, H], F32, SK)
    hn_s = d("hn_s", [nQT, 128, H], BF16, SK)
    attn_dbg = d("attn_dbg", [128, NH * (S // 2)], BF16, SK) if debug else None
    p_dbg = d("p_dbg", [4, 128, S], BF16, SK) if debug else None
    vt_dbg = d("vt_dbg", [128, (S // 128) * 144], BF16, SK) if debug else None
    av_dbg = d("av_dbg", [2, 128, 129], F32, SK) if debug else None

    with contextlib.ExitStack() as top:
        E_ = top.enter_context
        sb = lambda n, shp, dt: E_(nc.sbuf_tensor(n, shp, dt))
        em = Emitter(nc, top)

        psA = E_(nc.psum_tensor("psA", [128, 1024], F32))
        psB = E_(nc.psum_tensor("psB", [128, 1024], F32))
        psC = E_(nc.psum_tensor("psC", [128, 512], F32))
        psD = E_(nc.psum_tensor("psD", [128, 512], F32))
        psT2 = E_(nc.psum_tensor("psT2", [128, 1024], BF16))
        psT = E_(nc.psum_tensor("psT", [128, 1024], BF16))

        bqkv_sb = sb("bqkv_sb", [128, 48], F32)
        sbq_sb = sb("sbq_sb", [128, 16], F32)
        mask_sb = sb("mask_sb", [128, 256], F32)
        eps_sb = sb("eps_sb", [128, 1], F32)
        ones1 = sb("ones1", [1, 128], BF16)
        r_bqkv, r_sbq, r_mask, r_eps = Res(), Res(), Res(), Res()
        r_ones = Res()
        em.dma("y", lambda E: E.dma_start(out=bqkv_sb[:], in_=bqkv_pc), writes=[r_bqkv])
        em.dma("y", lambda E: E.dma_start(out=mask_sb[:], in_=mask_in), writes=[r_mask])
        em.emit("v", lambda E: E.memset(eps_sb[:], 1e-5), writes=[r_eps])
        em.emit("v", lambda E: E.memset(ones1[:], 1.0), writes=[r_ones])
        em.emit("v", lambda E: E.tensor_scalar_mul(sbq_sb[:], bqkv_sb[:, 0:16], SCL),
                reads=[r_bqkv], writes=[r_sbq])
        ident_bf = sb("ident_bf", [128, 128], BF16)
        ident512 = sb("ident512", [128, 512], BF16)
        maskt_sb = sb("maskt_sb", [128, 256], F32)
        r_ident, r_id512, r_maskt = Res(), Res(), Res()
        em.dma("g", lambda E: E.dma_start(out=ident_bf[:], in_=ident_in), writes=[r_ident])
        em.dma("y", lambda E: E.dma_start(out=maskt_sb[:], in_=maskt_in), writes=[r_maskt])
        for j4 in range(4):
            em.emit("v", lambda E, j4=j4: E.tensor_copy(ident512[:, j4 * 128:(j4 + 1) * 128],
                                                        ident_bf[:]),
                    reads=[r_ident], writes=[r_id512])

        r_psT = Res()
        r_psT2 = Res()
        tp_rr = [0]

        def emit_tpose(srcs, dsts, reads, writes, batch_dsts=None):
            n = len(srcs)
            for bi, b0 in enumerate(range(0, n, 8)):
                blk = list(range(b0, min(b0 + 8, n)))
                k = len(blk)
                tb = psT if tp_rr[0] % 2 == 0 else psT2
                r_tb = r_psT if tp_rr[0] % 2 == 0 else r_psT2
                tp_rr[0] += 1
                em.emit_group("t", [
                    (lambda E, j=j, i=i, tb=tb: E.transpose(tb[:, i * 128:(i + 1) * 128], srcs[j],
                                                            ident_bf[:]))
                    for i, j in enumerate(blk)
                ], reads=list(reads) + [r_ident], writes=[r_tb])
                if batch_dsts is not None:
                    em.emit("s", lambda E, bi=bi, k=k, tb=tb: E.activation(
                        batch_dsts[bi],
                        tb[:, 0:k * 128].rearrange("p (i s) -> p i s", i=k),
                        AF.Identity),
                        reads=[r_tb], writes=list(writes))
                else:
                    for i, j in enumerate(blk):
                        em.emit("s", lambda E, j=j, i=i, tb=tb: E.activation(
                            dsts[j], tb[:, i * 128:(i + 1) * 128], AF.Identity),
                            reads=[r_tb], writes=list(writes))

        def emit_ln(xa, r_xa, xn, r_xn, smt, r_sm, bnb, r_bnb):
            for i in range(4):
                em.emit("v", lambda E, i=i: E.bn_stats(bnb[:, i * 6:(i + 1) * 6],
                                                       xa[:, i * 512:(i + 1) * 512]),
                        reads=[r_xa], writes=[r_bnb])
            em.emit("v", lambda E: E.bn_aggr(smt[:, 0:2], bnb[:]), reads=[r_bnb], writes=[r_sm])
            em.emit("s", lambda E: E.activation(smt[:, 2:3], smt[:, 1:2], AF.Ln,
                                                bias=eps_sb[:], scale=1.0),
                    reads=[r_sm, r_eps], writes=[r_sm])
            em.emit("s", lambda E: E.activation(smt[:, 3:4], smt[:, 2:3], AF.Exp,
                                                bias=0.0, scale=-0.5),
                    reads=[r_sm], writes=[r_sm])
            em.emit("v", lambda E: E.tensor_scalar(xn[:], xa[:], smt[:, 0:1], smt[:, 3:4],
                                                   ALU.subtract, ALU.mult),
                    reads=[r_xa, r_sm], writes=[r_xn])

        p12 = contextlib.ExitStack()
        sb12 = lambda n, shp, dt: p12.enter_context(nc.sbuf_tensor(n, shp, dt))
        xnow = sb12("xnow", [128, IC * OWN], BF16)
        xt_12 = [sb12(f"xt12_{k}", [128, H], F32) for k in range(2)]
        xn_12 = [sb12(f"xn12_{k}", [128, H], BF16) for k in range(2)]
        sm_12 = [sb12(f"sm12_{k}", [128, 8], F32) for k in range(2)]
        bnb_12 = [sb12(f"bnb12_{k}", [128, 24], F32) for k in range(2)]

        r_xt = [Res(), Res()]
        r_xn = [Res(), Res()]
        r_sm = [Res(), Res()]
        r_bnb = [Res(), Res()]
        r_xnow = [Res() for _ in range(nQT)]
        r_qfs = [Res() for _ in range(NH)]
        r_kfs = [Res() for _ in range(NH)]
        r_vfs = [Res() for _ in range(NH)]

        def ln_tile12(t):
            k = t % 2
            em.dma("y", lambda E, t=t, k=k: E.dma_start(out=xt_12[k][:],
                                                        in_=x_own[t * 128:(t + 1) * 128, :]),
                   writes=[r_xt[k]])
            emit_ln(xt_12[k], r_xt[k], xn_12[k], r_xn[k], sm_12[k], r_sm[k], bnb_12[k], r_bnb[k])
            xv = xnow[:].rearrange("p (i s) -> p i s", i=IC)
            emit_tpose([xn_12[k][:, i * 128:(i + 1) * 128] for i in range(IC)], None,
                       [r_xn[k]], [r_xnow[t]],
                       batch_dsts=[xv[:, 0:8, t * 128:(t + 1) * 128],
                                   xv[:, 8:16, t * 128:(t + 1) * 128]])

        for t in range(4):
            ln_tile12(t)
        nOQ = OWN // 512
        ps4 = [psA[:, 0:512], psA[:, 512:1024], psB[:, 0:512], psB[:, 512:1024]]
        r_ps4 = [Res() for _ in range(4)]
        qst = [sb12(f"qst{j}", [128, 512], BF16) for j in range(4)]
        r_qst = [Res() for _ in range(4)]
        wq8 = [sb12(f"wq8_{j}", [128, 2048], BF16) for j in range(16)]
        r_wq8 = [Res() for _ in range(16)]
        for o in range(16):
            em.dma("g", lambda E, o=o: E.dma_start(out=wq8[o][:], in_=wqkv_t[o]),
                   writes=[r_wq8[o]])
        for n in range(nOQ):
            for t in range(4 + 4 * (n - 1), 4 + 4 * n):
                if 0 <= t < nQT:
                    ln_tile12(t)
            rdt = [r_xnow[tt] for tt in range(n * 4, n * 4 + 4)]
            for o in range(16):
                pj = (n * 16 + o) % 4
                em.emit_group("t", [
                    (lambda E, i=i, o=o, n=n, pj=pj: E.matmul(
                        ps4[pj], wq8[o][:, i * 128:(i + 1) * 128],
                        xnow[:, i * OWN + n * 512: i * OWN + (n + 1) * 512],
                        start=(i == 0), stop=(i == IC - 1)))
                    for i in range(IC)
                ], reads=[r_wq8[o]] + rdt, writes=[r_ps4[pj]])
                sj = (n * 16 + o) % 4
                em.emit("s", lambda E, o=o, pj=pj, sj=sj: E.activation(
                    qst[sj][:], ps4[pj], AF.Identity,
                    bias=sbq_sb[:, o:o + 1], scale=SCL),
                    reads=[r_ps4[pj], r_sbq], writes=[r_qst[sj]])
                em.dma("y", lambda E, o=o, n=n, sj=sj: E.dma_start(
                    out=qf_s[o][:, n * 512:(n + 1) * 512], in_=qst[sj][:]),
                    reads=[r_qst[sj]], writes=[r_qfs[o]])

        em.fence()
        p12.close()

        p34 = contextlib.ExitStack()
        sb34 = lambda n, shp, dt: p34.enter_context(nc.sbuf_tensor(n, shp, dt))
        xnf_r = sb34("xnf_r", [128, IC * 1024], BF16)
        xt_34 = [sb34(f"xt34_{k}", [128, H], F32) for k in range(2)]
        xn_34 = [sb34(f"xn34_{k}", [128, H], BF16) for k in range(2)]
        sm_34 = [sb34(f"sm34_{k}", [128, 8], F32) for k in range(2)]
        bnb_34 = [sb34(f"bnb34_{k}", [128, 24], F32) for k in range(2)]

        r_xt34 = [Res(), Res()]
        r_xn34 = [Res(), Res()]
        r_sm34 = [Res(), Res()]
        r_bnb34 = [Res(), Res()]
        r_xnfr = [Res() for _ in range(8)]

        def ln_tile34(t):
            k = t % 2
            em.dma("y", lambda E, t=t, k=k: E.dma_start(out=xt_34[k][:],
                                                        in_=x_seq[t * 128:(t + 1) * 128, :]),
                   writes=[r_xt34[k]])
            emit_ln(xt_34[k], r_xt34[k], xn_34[k], r_xn34[k], sm_34[k], r_sm34[k],
                    bnb_34[k], r_bnb34[k])
            xfv = xnf_r[:].rearrange("p (i s) -> p i s", i=IC)
            ts = t % 8
            emit_tpose([xn_34[k][:, i * 128:(i + 1) * 128] for i in range(IC)], None,
                       [r_xn34[k]], [r_xnfr[ts]],
                       batch_dsts=[xfv[:, 0:8, ts * 128:(ts + 1) * 128],
                                   xfv[:, 8:16, ts * 128:(ts + 1) * 128]])

        for t in range(8):
            ln_tile34(t)

        nSQ = S // 512
        kst = [sb34(f"kst{j}", [128, 512], BF16) for j in range(4)]
        r_kst = [Res() for _ in range(4)]
        wkv = [sb34(f"wkv{o}", [128, 2048], BF16) for o in range(32)]
        r_wkv = [Res() for _ in range(32)]
        for o in range(16, 48):
            em.dma("g", lambda E, o=o: E.dma_start(out=wkv[o - 16][:], in_=wqkv_t[o]),
                   writes=[r_wkv[o - 16]])
        for n in range(nSQ):
            rdt = [r_xnfr[tt % 8] for tt in range(n * 4, n * 4 + 4)]
            c0 = (n % 2) * 512
            for oo in range(32):
                o = 16 + oo
                pj = (n * 32 + oo) % 4
                em.emit_group("t", [
                    (lambda E, i=i, oo=oo, c0=c0, pj=pj: E.matmul(
                        ps4[pj], wkv[oo][:, i * 128:(i + 1) * 128],
                        xnf_r[:, i * 1024 + c0: i * 1024 + c0 + 512],
                        start=(i == 0), stop=(i == IC - 1)))
                    for i in range(IC)
                ], reads=[r_wkv[oo]] + rdt, writes=[r_ps4[pj]])
                sj = (n * 32 + oo) % 4
                em.emit("s", lambda E, o=o, pj=pj, sj=sj: E.activation(
                    kst[sj][:], ps4[pj], AF.Identity,
                    bias=bqkv_sb[:, o:o + 1], scale=1.0),
                    reads=[r_ps4[pj], r_bqkv], writes=[r_kst[sj]])
                if o < 32:
                    em.dma("y", lambda E, o=o, n=n, sj=sj: E.dma_start(
                        out=kf_s[o - 16][:, n * 512:(n + 1) * 512], in_=kst[sj][:]),
                        reads=[r_kst[sj]], writes=[r_kfs[o - 16]])
                else:
                    em.dma("y", lambda E, o=o, n=n, sj=sj: E.dma_start(
                        out=vf_s[o - 32][:, n * 512:(n + 1) * 512], in_=kst[sj][:]),
                        reads=[r_kst[sj]], writes=[r_vfs[o - 32]])
            for t in range(8 + 4 * n, min(8 + 4 * n + 4, nTT)):
                ln_tile34(t)

        em.fence()
        p34.close()

        p56 = contextlib.ExitStack()
        attn_fm = p56.enter_context(nc.sbuf_tensor("attn_fm", [128, NH * OWN], BF16))
        p5 = contextlib.ExitStack()
        sb5 = lambda n, shp, dt: p5.enter_context(nc.sbuf_tensor(n, shp, dt))
        kf_sb = [sb5(f"kf_{k}", [128, S], BF16) for k in range(2)]
        qf_sb = [sb5(f"qf_{k}", [128, OWN], BF16) for k in range(2)]
        vfm_sb = [sb5(f"vfm_{k}", [128, S], BF16) for k in range(2)]
        vt_sb = [sb5(f"vt_{k}", [128, nKT * 129], BF16) for k in range(2)]
        pt_sb = [sb5(f"pt_{k}", [128, nKT * 512], BF16) for k in range(2)]
        at_sb = [sb5(f"at_{k}", [128, 128], BF16) for k in range(2)]
        sm5 = [sb5(f"sm5_{k}", [128, 8], F32) for k in range(2)]
        cmax = [sb5(f"cmax_{k}", [128, 40], F32) for k in range(2)]
        cmbf = [sb5(f"cmbf_{k}", [128, 1], BF16) for k in range(2)]
        mg_bf = [sb5(f"mg_{k}", [1, 1], BF16) for k in range(2)]
        bias_ap = [sb5(f"biasap_{k}", [128, 1], F32) for k in range(2)]

        r_kf = [Res(), Res()]
        r_qf = [Res(), Res()]
        r_vfm = [Res(), Res()]
        r_vt = [Res(), Res()]
        r_pt = [Res(), Res()]
        r_at = [Res(), Res()]
        r_sm5 = [Res(), Res()]
        r_cmax = [Res(), Res()]
        r_cmbf = [Res(), Res()]
        r_mg = [Res(), Res()]
        r_bias = [Res(), Res()]
        r_st4 = [Res() for _ in range(4)]
        r_av = [Res(), Res()]
        r_afm = {}
        st4 = [psA[:, 0:512], psA[:, 512:1024], psB[:, 0:512], psB[:, 512:1024]]

        for k in range(2):
            em.emit_group("v", [
                (lambda E, k=k, kt=kt: E.memset(vt_sb[k][:, kt * 129 + 128: kt * 129 + 129], 1.0))
                for kt in range(nKT)
            ], writes=[r_vt[k]])

        def load_head(h):
            hb = h % 2
            em.dma("y", lambda E, h=h, hb=hb: E.dma_start(out=kf_sb[hb][:], in_=kf_s[h]),
                   reads=[r_kfs[h]], writes=[r_kf[hb]])
            em.dma("y", lambda E, h=h, hb=hb: E.dma_start(out=qf_sb[hb][:], in_=qf_s[h]),
                   reads=[r_qfs[h]], writes=[r_qf[hb]])
            em.dma("g", lambda E, h=h, hb=hb: E.dma_start(out=vfm_sb[hb][:], in_=vf_s[h]),
                   reads=[r_vfs[h]], writes=[r_vfm[hb]])
            vv = vt_sb[hb][:].rearrange("p (k c) -> p k c", c=129)
            emit_tpose([vfm_sb[hb][:, kt * 128:(kt + 1) * 128] for kt in range(nKT)], None,
                       [r_vfm[hb]], [r_vt[hb]],
                       batch_dsts=[vv[:, b0:min(b0 + 8, nKT), 0:128]
                                   for b0 in range(0, nKT, 8)])

        load_head(0)
        load_head(1)

        nQG = nQT // 4
        slot_ctr = [0]

        def st_block(h, g, gb, kt, do_stats):
            hb = h % 2
            jmin = max(0, kt // 2 - 4 * g)
            N = 512 - jmin * 128
            sj = slot_ctr[0] % 4
            slot_ctr[0] += 1
            stb = st4[sj]
            r_st = r_st4[sj]
            em.emit("t", lambda E, kt=kt, g=g, hb=hb, jmin=jmin, N=N, stb=stb: E.matmul(
                stb[:, 0:N], kf_sb[hb][:, kt * 128:(kt + 1) * 128],
                qf_sb[hb][:, (4 * g + jmin) * 128:(4 * g + 4) * 128],
                start=True, stop=True),
                reads=[r_kf[hb], r_qf[hb]], writes=[r_st])
            if do_stats:
                em.emit("v", lambda E, N=N, kt=kt, stb=stb, gb=gb: E.reduce_max(
                    cmax[gb][:, kt:kt + 1], stb[:, 0:N], axis=AX.X),
                    reads=[r_st], writes=[r_cmax[gb]])
            else:
                if kt >= 8 * g:
                    mh = 0 if kt % 2 == 0 else 1
                    em.emit("v", lambda E, stb=stb, mh=mh: E.tensor_add(
                        stb[:, 0:128], stb[:, 0:128], maskt_sb[:, mh * 128:(mh + 1) * 128]),
                        reads=[r_maskt], writes=[r_st])
                em.emit("s", lambda E, kt=kt, N=N, stb=stb, gb=gb: E.activation(
                    pt_sb[gb][:, kt * 512: kt * 512 + N], stb[:, 0:N], AF.Exp,
                    bias=bias_ap[gb][:], scale=1.0),
                    reads=[r_st, r_bias[gb]], writes=[r_pt[gb]])

        def st_mms(h, g, gb, do_stats, inter=None):
            nktm = 8 * g + 8
            pk = 0
            pn = 8 * inter[1] + 8 if inter is not None else 0
            for kt in range(nktm):
                st_block(h, g, gb, kt, do_stats)
                if inter is not None and pk < pn:
                    st_block(inter[0], inter[1], inter[2], pk, False)
                    pk += 1
            while inter is not None and pk < pn:
                st_block(inter[0], inter[1], inter[2], pk, False)
                pk += 1

        def m_chain(h, g, gb):
            nktm = 8 * g + 8
            em.emit("v", lambda E, gb=gb, nktm=nktm: E.reduce_max(
                sm5[gb][:, 0:1], cmax[gb][:, 0:nktm], axis=AX.X),
                reads=[r_cmax[gb]], writes=[r_sm5[gb]])
            em.emit("v", lambda E, gb=gb: E.tensor_copy(cmbf[gb][:], sm5[gb][:, 0:1]),
                    reads=[r_sm5[gb]], writes=[r_cmbf[gb]])
            em.emit("t", lambda E, gb=gb: E.matmul(psC[0:1, 256:384], cmbf[gb][:], ident_bf[:],
                                                   start=True, stop=True),
                    reads=[r_cmbf[gb], r_ident], writes=[r_av[0]])
            em.emit("v", lambda E, gb=gb: E.reduce_max(sm5[gb][0:1, 2:3], psC[0:1, 256:384], axis=AX.X),
                    reads=[r_av[0]], writes=[r_sm5[gb]])
            em.emit("v", lambda E, gb=gb: E.tensor_copy(mg_bf[gb][:], sm5[gb][0:1, 2:3]),
                    reads=[r_sm5[gb]], writes=[r_mg[gb]])
            em.emit("t", lambda E, gb=gb: E.matmul(psC[:, 384:385], ones1[:], mg_bf[gb][:],
                                                   start=True, stop=True),
                    reads=[r_ones, r_mg[gb]], writes=[r_av[0]])
            em.emit("v", lambda E, gb=gb: E.tensor_scalar_mul(bias_ap[gb][:], psC[:, 384:385], -1.0),
                    reads=[r_av[0]], writes=[r_bias[gb]])

        def av_evac(h, g, gb, j):
            qt = 4 * g + j
            ab = j % 2
            av = psC[:, 0:129] if ab == 0 else psD[:, 0:129]
            r_a = r_av[ab]
            em.emit("v", lambda E, gb=gb, av=av, ab=ab: E.tensor_scalar_max(
                sm5[gb][:, 4 + ab:5 + ab], av[:, 128:129], 1e-35),
                reads=[r_a], writes=[r_sm5[gb]])
            em.emit("v", lambda E, gb=gb, ab=ab: E.reciprocal(
                sm5[gb][:, 6 + ab:7 + ab], sm5[gb][:, 4 + ab:5 + ab]),
                reads=[r_sm5[gb]], writes=[r_sm5[gb]])
            em.emit("v", lambda E, gb=gb, av=av, ab=ab: E.tensor_scalar_mul(
                at_sb[ab][:], av[:, 0:128], sm5[gb][:, 6 + ab:7 + ab]),
                reads=[r_a, r_sm5[gb]], writes=[r_at[ab]])
            r_afm[(h, qt)] = Res()
            emit_tpose([at_sb[ab][:]],
                       [attn_fm[:, h * OWN + qt * 128: h * OWN + (qt + 1) * 128]],
                       [r_at[ab]], [r_afm[(h, qt)]])

        def av_group(h, g, gb):
            hb = h % 2
            for j in range(4):
                qt = 4 * g + j
                nkt = 2 * (qt + 1)
                ab = j % 2
                av = psC[:, 0:129] if ab == 0 else psD[:, 0:129]
                r_a = r_av[ab]
                fns = []
                for kt in range(nkt):
                    jmin = max(0, kt // 2 - 4 * g)
                    jr = j - jmin
                    fns.append(lambda E, kt=kt, jr=jr, gb=gb, hb=hb, av=av, nkt=nkt: E.matmul(
                        av, pt_sb[gb][:, kt * 512 + jr * 128: kt * 512 + (jr + 1) * 128],
                        vt_sb[hb][:, kt * 129: kt * 129 + 129],
                        start=(kt == 0), stop=(kt == nkt - 1)))
                em.emit_group("t", fns, reads=[r_pt[gb], r_vt[hb]], writes=[r_a])
                if j > 0:
                    av_evac(h, g, gb, j - 1)
            av_evac(h, g, gb, 3)

        prev = None
        for h in range(NH):
            for g in range(nQG):
                gb = (h * nQG + g) % 2
                st_mms(h, g, gb, do_stats=True, inter=prev)
                if prev is not None:
                    av_group(prev[0], prev[1], prev[2])
                m_chain(h, g, gb)
                prev = (h, g, gb)
            ph, pg, pgb = prev
            st_mms(ph, pg, pgb, do_stats=False)
            av_group(ph, pg, pgb)
            prev = None
            if h + 2 < NH:
                load_head(h + 2)

        if debug:
            r_all = [r_afm[(hh, qq)] for hh in range(NH) for qq in range(nQT)]
            em.dma("y", lambda E: E.dma_start(out=attn_dbg, in_=attn_fm[:]), reads=r_all)

        em.fence()
        p5.close()

        p6 = contextlib.ExitStack()
        sb6 = lambda n, shp, dt: p6.enter_context(nc.sbuf_tensor(n, shp, dt))
        wout_sb = sb6("wout_sb", [128, 16 * H], BF16)
        bout_f = sb6("bout_f", [1, H], F32)
        bout_bf = sb6("bout_bf", [1, H], BF16)
        r_boutf, r_boutbf = Res(), Res()
        em.dma("y", lambda E: E.dma_start(out=bout_f[:], in_=bout_row), writes=[r_boutf])
        em.emit("v", lambda E: E.tensor_copy(bout_bf[:], bout_f[:]),
                reads=[r_boutf], writes=[r_boutbf])
        xt_6 = [sb6(f"xt6_{k}", [128, H], F32) for k in range(2)]
        ht_6 = [sb6(f"ht6_{k}", [128, H], F32) for k in range(2)]
        hn_6 = [sb6(f"hn6_{k}", [128, H], BF16) for k in range(2)]
        sm_6 = [sb6(f"sm6_{k}", [128, 8], F32) for k in range(2)]
        bnb_6 = [sb6(f"bnb6_{k}", [128, 24], F32) for k in range(2)]

        r_wout = [Res() for _ in range(16)]
        r_xt6 = [Res(), Res()]
        r_ht6 = [Res(), Res()]
        r_hn6 = [Res(), Res()]
        r_sm6 = [Res(), Res()]
        r_bnb6 = [Res(), Res()]
        r_hs = [Res() for _ in range(nQT)]
        r_hns = [Res() for _ in range(nQT)]

        for i in range(16):
            em.dma("g", lambda E, i=i: E.dma_start(out=wout_sb[:, i * H:(i + 1) * H], in_=wout_t[i]),
                   writes=[r_wout[i]])

        for qt in range(nQT):
            k = qt % 2
            em.dma("y", lambda E, qt=qt, k=k: E.dma_start(out=xt_6[k][:],
                                                          in_=x_own[qt * 128:(qt + 1) * 128, :]),
                   writes=[r_xt6[k]])
            for oc in range(4):
                pj = (qt * 4 + oc) % 4
                fns = [lambda E, oc=oc, pj=pj: E.matmul(
                    ps4[pj], ones1[:], bout_bf[0:1, oc * 512:(oc + 1) * 512],
                    start=True, stop=False)]
                fns += [
                    (lambda E, i=i, qt=qt, oc=oc, pj=pj: E.matmul(
                        ps4[pj], attn_fm[:, i * OWN + qt * 128: i * OWN + (qt + 1) * 128],
                        wout_sb[:, i * H + oc * 512: i * H + oc * 512 + 512],
                        start=False, stop=(i == 15)))
                    for i in range(16)
                ]
                em.emit_group("t", fns,
                              reads=[r_ones, r_boutbf] +
                                    [r_afm[(i, qt)] for i in range(NH)] + r_wout,
                              writes=[r_ps4[pj]])
                em.emit("v", lambda E, oc=oc, k=k, pj=pj: E.tensor_add(
                    ht_6[k][:, oc * 512:(oc + 1) * 512], ps4[pj], xt_6[k][:, oc * 512:(oc + 1) * 512]),
                    reads=[r_ps4[pj], r_xt6[k]], writes=[r_ht6[k]])
            em.dma("y", lambda E, qt=qt, k=k: E.dma_start(out=h_s[qt], in_=ht_6[k][:]),
                   reads=[r_ht6[k]], writes=[r_hs[qt]])
            emit_ln(ht_6[k], r_ht6[k], hn_6[k], r_hn6[k], sm_6[k], r_sm6[k], bnb_6[k], r_bnb6[k])
            em.dma("y", lambda E, qt=qt, k=k: E.dma_start(out=hn_s[qt], in_=hn_6[k][:]),
                   reads=[r_hn6[k]], writes=[r_hns[qt]])

        em.fence()
        p6.close()
        p56.close()

        p7 = contextlib.ExitStack()
        sb7 = lambda n, shp, dt: p7.enter_context(nc.sbuf_tensor(n, shp, dt))
        TPP = nQT // 2
        hnf = sb7("hnf", [128, IC * TPP * 128], BF16)
        y_sb = sb7("y_sb", [128, TPP * H], F32)
        slot = [sb7(f"slot{j}", [128, 8192], BF16) for j in range(3)]
        gate_bf = sb7("gate_bf", [128, 128], BF16)
        comb = sb7("comb", [128, TPP * 8], F32)
        lgt = sb7("lgt", [128, 16], F32)
        sm7 = sb7("sm7", [128, 8], F32)
        sil_7 = [sb7(f"sil7_{k}", [128, 512], F32) for k in range(2)]
        td_7 = [sb7(f"td7_{k}", [128, 512], BF16) for k in range(2)]
        tdf_7 = [sb7(f"tdf7_{k}", [128, 512], BF16) for k in range(2)]
        tdfs = [sb7(f"tdfs_{k}", [128, TPP * 512], BF16) for k in range(2)]
        ht_7 = [sb7(f"ht7_{k}", [128, H], F32) for k in range(2)]
        hnt_7 = [sb7(f"hnt7_{k}", [128, H], BF16) for k in range(2)]
        r_hnt7 = [Res(), Res()]
        ot_7 = [sb7("ot7_0", [128, H], F32)]

        r_slot = [Res() for _ in range(3)]
        r_gate = Res()
        r_comb = Res()
        r_lgt = Res()
        r_sm7 = Res()
        r_sil = [Res(), Res()]
        r_td = [Res(), Res()]
        r_tdf = [Res(), Res()]
        r_tdfs = [Res(), Res()]
        r_hnf = [Res() for _ in range(TPP)]
        r_ht7 = [Res(), Res()]
        r_ot7 = [Res()]
        r_y = Res()
        st_toks = []

        em.dma("g", [
            (lambda E, i=i: E.dma_start(out=gate_bf[:, i * 8:(i + 1) * 8], in_=gate_wt[i]))
            for i in range(16)
        ], writes=[r_gate])

        slot_rr = [0]

        def load_slot(fns, rds=()):
            j = slot_rr[0]
            slot_rr[0] = (j + 1) % 3
            em.dma("g", [
                (lambda E, fn=fn, j=j: fn(E, slot[j])) for fn in fns
            ], reads=list(rds), writes=[r_slot[j]])
            return j

        def wg_loader(src, e):
            return [(lambda E, S_, src=src, e=e: E.dma_start(out=S_[:], in_=src[e]))]

        def wd_loader(e):
            return [(lambda E, S_, e=e: E.dma_start(out=S_[:], in_=wd[e]))]

        def sw_loader(src, half):
            return [(lambda E, S_, src=src, half=half: E.dma_start(out=S_[:], in_=src[half]))]

        def swd_loader(half):
            return [(lambda E, S_, half=half: E.dma_start(out=S_[:], in_=swd[half]))]

        y_ps = [psA[:, 0:512], psA[:, 512:1024]]
        r_yps = [Res(), Res()]
        g_ps2 = [psC, psC]
        u_ps2 = [psD, psB[:, 512:1024]]
        r_gc = Res()
        r_gps2 = [r_gc, r_gc]
        r_ups2 = [Res(), Res()]
        lg_ps = psB[:, 0:512]
        r_lgps = Res()

        for pa in range(2):
            t0 = pa * TPP
            for t in range(TPP):
                kb = t % 2
                em.dma("y", lambda E, t=t, pa=pa, kb=kb: E.dma_start(
                    out=hnt_7[kb][:], in_=hn_s[pa * TPP + t]),
                    reads=[r_hns[pa * TPP + t]], writes=[r_hnt7[kb]])
                hv = hnf[:].rearrange("p (i s) -> p i s", i=IC)
                emit_tpose([hnt_7[kb][:, i * 128:(i + 1) * 128] for i in range(IC)], None,
                           [r_hnt7[kb]], [r_hnf[t]],
                           batch_dsts=[hv[:, 0:8, t * 128:(t + 1) * 128],
                                       hv[:, 8:16, t * 128:(t + 1) * 128]])
            em.emit_group("v", [
                (lambda E, t=t: E.memset(y_sb[:, t * H:(t + 1) * H], 0.0)) for t in range(TPP)
            ], writes=[r_y])

            for t in range(TPP):
                em.emit_group("t", [
                    (lambda E, i=i, t=t: E.matmul(
                        lg_ps[:, 0:8], hnf[:, i * TPP * 128 + t * 128: i * TPP * 128 + (t + 1) * 128],
                        gate_bf[:, i * 8:(i + 1) * 8], start=(i == 0), stop=(i == IC - 1)))
                    for i in range(IC)
                ], reads=[r_gate, r_hnf[t]], writes=[r_lgps])
                em.emit("v", lambda E: E.tensor_copy(lgt[:, 0:8], lg_ps[:, 0:8]),
                        reads=[r_lgps], writes=[r_lgt])
                em.emit("v", lambda E: E.reduce_max(sm7[:, 0:1], lgt[:, 0:8], axis=AX.X),
                        reads=[r_lgt], writes=[r_sm7])
                em.emit("v", lambda E: E.tensor_scalar(lgt[:, 8:16], lgt[:, 0:8], sm7[:, 0:1],
                                                       None, ALU.is_equal),
                        reads=[r_lgt, r_sm7], writes=[r_lgt])
                em.emit("v", lambda E, t=t: E.scalar_tensor_tensor(
                    comb[:, t * 8:(t + 1) * 8], lgt[:, 8:16], -1e30, lgt[:, 0:8],
                    ALU.mult, ALU.add), reads=[r_lgt], writes=[r_comb])
                em.emit("v", lambda E, t=t: E.reduce_max(sm7[:, 1:2], comb[:, t * 8:(t + 1) * 8],
                                                         axis=AX.X),
                        reads=[r_comb], writes=[r_sm7])
                em.emit("v", lambda E: E.tensor_sub(sm7[:, 2:3], sm7[:, 1:2], sm7[:, 0:1]),
                        reads=[r_sm7], writes=[r_sm7])
                em.emit("s", lambda E: E.activation(sm7[:, 3:4], sm7[:, 2:3], AF.Exp,
                                                    bias=0.0, scale=1.0),
                        reads=[r_sm7], writes=[r_sm7])
                em.emit("v", lambda E: E.tensor_scalar(sm7[:, 4:5], sm7[:, 3:4], 1.0, None, ALU.add),
                        reads=[r_sm7], writes=[r_sm7])
                em.emit("v", lambda E: E.reciprocal(sm7[:, 5:6], sm7[:, 4:5]),
                        reads=[r_sm7], writes=[r_sm7])
                em.emit("v", lambda E: E.tensor_mul(sm7[:, 6:7], sm7[:, 3:4], sm7[:, 5:6]),
                        reads=[r_sm7], writes=[r_sm7])
                em.emit("v", lambda E, t=t: E.tensor_scalar(
                    comb[:, t * 8:(t + 1) * 8], lgt[:, 0:8], sm7[:, 1:2], sm7[:, 6:7],
                    ALU.is_equal, ALU.mult), reads=[r_lgt, r_sm7], writes=[r_comb])
                em.emit("v", lambda E, t=t: E.tensor_scalar(
                    lgt[:, 8:16], lgt[:, 0:8], sm7[:, 0:1], sm7[:, 5:6],
                    ALU.is_equal, ALU.mult), reads=[r_lgt, r_sm7], writes=[r_lgt])
                em.emit("v", lambda E, t=t: E.tensor_add(
                    comb[:, t * 8:(t + 1) * 8], comb[:, t * 8:(t + 1) * 8], lgt[:, 8:16]),
                    reads=[r_lgt], writes=[r_comb])

            def gut_part(t, e, jg, ju):
                kb = t % 2
                g_ps, u_ps = g_ps2[kb], u_ps2[kb]
                r_gps, r_ups = r_gps2[kb], r_ups2[kb]
                em.emit_group("t", [
                    (lambda E, i=i, t=t, jg=jg: E.matmul(
                        g_ps[:], hnf[:, i * TPP * 128 + t * 128: i * TPP * 128 + (t + 1) * 128],
                        slot[jg][:, i * 512:(i + 1) * 512], start=(i == 0), stop=(i == IC - 1)))
                    for i in range(IC)
                ], reads=[r_slot[jg], r_hnf[t]], writes=[r_gps])
                em.emit_group("t", [
                    (lambda E, i=i, t=t, ju=ju: E.matmul(
                        u_ps[:], hnf[:, i * TPP * 128 + t * 128: i * TPP * 128 + (t + 1) * 128],
                        slot[ju][:, i * 512:(i + 1) * 512], start=(i == 0), stop=(i == IC - 1)))
                    for i in range(IC)
                ], reads=[r_slot[ju], r_hnf[t]], writes=[r_ups])
                em.emit("s", lambda E, kb=kb: E.activation(sil_7[kb][:], g_ps[:], AF.Silu),
                        reads=[r_gps], writes=[r_sil[kb]])
                if e is not None:
                    em.emit("v", lambda E, kb=kb, t=t, e=e: E.scalar_tensor_tensor(
                        td_7[kb][:], sil_7[kb][:], comb[:, t * 8 + e: t * 8 + e + 1], u_ps[:],
                        ALU.mult, ALU.mult),
                        reads=[r_sil[kb], r_comb, r_ups], writes=[r_td[kb]])
                else:
                    em.emit("v", lambda E, kb=kb: E.tensor_mul(td_7[kb][:], sil_7[kb][:], u_ps[:]),
                            reads=[r_sil[kb], r_ups], writes=[r_td[kb]])
            def tp_part(t):
                kb = t % 2
                emit_tpose([td_7[kb][:, mc * 128:(mc + 1) * 128] for mc in range(4)], None,
                           [r_td[kb]], [r_tdf[kb]],
                           batch_dsts=[tdf_7[kb][:].rearrange("p (i s) -> p i s", i=4)])

            def y_part(t, jd):
                kb = t % 2
                for oc in range(4):
                    yj = (t * 4 + oc) % 2
                    em.emit_group("t", [
                        (lambda E, mc=mc, oc=oc, kb=kb, jd=jd, yj=yj: E.matmul(
                            y_ps[yj], tdf_7[kb][:, mc * 128:(mc + 1) * 128],
                            slot[jd][:, mc * 2048 + oc * 512: mc * 2048 + oc * 512 + 512],
                            start=(mc == 0), stop=(mc == 3)))
                        for mc in range(4)
                    ], reads=[r_tdf[kb], r_slot[jd]], writes=[r_yps[yj]])
                    em.emit("v", lambda E, t=t, oc=oc, yj=yj: E.scalar_tensor_tensor(
                        y_sb[:, t * H + oc * 512: t * H + oc * 512 + 512], y_ps[yj], 1.0,
                        y_sb[:, t * H + oc * 512: t * H + oc * 512 + 512], ALU.mult, ALU.add),
                        reads=[r_yps[yj]], writes=[r_y])

            for e in range(NEXP):
                jg = load_slot(wg_loader(wg, e))
                ju = load_slot(wg_loader(wu, e))
                jd = load_slot(wd_loader(e))
                prev = None
                for t in range(TPP):
                    gut_part(t, e, jg, ju)
                    if prev is not None:
                        y_part(prev, jd)
                    tp_part(t)
                    prev = t
                y_part(prev, jd)

            for half in range(2):
                jg = load_slot(sw_loader(swg, half))
                ju = load_slot(sw_loader(swu, half))
                tp_pend = []
                for t in range(TPP):
                    kb = t % 2
                    g_ps, u_ps = g_ps2[kb], u_ps2[kb]
                    r_gps, r_ups = r_gps2[kb], r_ups2[kb]
                    em.emit_group("t", [
                        (lambda E, i=i, t=t, jg=jg, g_ps=g_ps: E.matmul(
                            g_ps[:], hnf[:, i * TPP * 128 + t * 128: i * TPP * 128 + (t + 1) * 128],
                            slot[jg][:, i * 512:(i + 1) * 512], start=(i == 0), stop=(i == IC - 1)))
                        for i in range(IC)
                    ], reads=[r_slot[jg], r_hnf[t]], writes=[r_gps])
                    em.emit_group("t", [
                        (lambda E, i=i, t=t, ju=ju, u_ps=u_ps: E.matmul(
                            u_ps[:], hnf[:, i * TPP * 128 + t * 128: i * TPP * 128 + (t + 1) * 128],
                            slot[ju][:, i * 512:(i + 1) * 512], start=(i == 0), stop=(i == IC - 1)))
                        for i in range(IC)
                    ], reads=[r_slot[ju], r_hnf[t]], writes=[r_ups])
                    em.emit("s", lambda E, kb=kb, g_ps=g_ps: E.activation(sil_7[kb][:], g_ps[:], AF.Silu),
                            reads=[r_gps], writes=[r_sil[kb]])
                    em.emit("v", lambda E, kb=kb, u_ps=u_ps: E.tensor_mul(td_7[kb][:], sil_7[kb][:], u_ps[:]),
                            reads=[r_sil[kb], r_ups], writes=[r_td[kb]])
                    tp_pend.append((t, kb))
                    if len(tp_pend) > 1:
                        pt_, pkb = tp_pend.pop(0)
                        emit_tpose([td_7[pkb][:, mc * 128:(mc + 1) * 128] for mc in range(4)], None,
                                   [r_td[pkb]], [r_tdfs[half]],
                                   batch_dsts=[tdfs[half][:, pt_ * 512:(pt_ + 1) * 512]
                                               .rearrange("p (i s) -> p i s", i=4)])
                while tp_pend:
                    pt_, pkb = tp_pend.pop(0)
                    emit_tpose([td_7[pkb][:, mc * 128:(mc + 1) * 128] for mc in range(4)], None,
                               [r_td[pkb]], [r_tdfs[half]],
                               batch_dsts=[tdfs[half][:, pt_ * 512:(pt_ + 1) * 512]
                                           .rearrange("p (i s) -> p i s", i=4)])
            for half in range(2):
                jd = load_slot(swd_loader(half))
                for t in range(TPP):
                    for oc in range(4):
                        yj = (t * 4 + oc) % 2
                        em.emit_group("t", [
                            (lambda E, mc=mc, oc=oc, t=t, jd=jd, half=half, yj=yj: E.matmul(
                                y_ps[yj], tdfs[half][:, t * 512 + mc * 128: t * 512 + (mc + 1) * 128],
                                slot[jd][:, mc * 2048 + oc * 512: mc * 2048 + oc * 512 + 512],
                                start=(mc == 0), stop=(mc == 3)))
                            for mc in range(4)
                        ], reads=[r_tdfs[half], r_slot[jd]], writes=[r_yps[yj]])
                        em.emit("v", lambda E, t=t, oc=oc, yj=yj: E.scalar_tensor_tensor(
                            y_sb[:, t * H + oc * 512: t * H + oc * 512 + 512], y_ps[yj], 1.0,
                            y_sb[:, t * H + oc * 512: t * H + oc * 512 + 512], ALU.mult, ALU.add),
                            reads=[r_yps[yj]], writes=[r_y])

            for t in range(TPP):
                k = t % 2
                em.dma("y", lambda E, t=t, pa=pa, k=k: E.dma_start(out=ht_7[k][:],
                                                                  in_=h_s[pa * TPP + t]),
                       reads=[r_hs[pa * TPP + t]], writes=[r_ht7[k]])
                em.emit("v", lambda E, t=t, k=k: E.tensor_add(
                    ot_7[0][:], y_sb[:, t * H:(t + 1) * H], ht_7[k][:]),
                    reads=[r_y, r_ht7[k]], writes=[r_ot7[0]])
                gt = pa * TPP + t
                st_toks.append(em.dma("y", lambda E, gt=gt: E.dma_start(
                    out=out[gt * 128:(gt + 1) * 128, :], in_=ot_7[0][:]),
                    reads=[r_ot7[0]]))
            if pa == 0:
                em.fence()

        em.wait_tokens("y", st_toks)
        p7.close()

        block = E_(nc.Block())
        em.emit_into_blocks(block)
    return nc


def host_prep(X, W_qkv, b_qkv, W_out, b_out, gate_w, Wg, Wu, Wd, sWg, sWu, sWd):
    X = np.asarray(X, dtype=np.float32)
    B_, S_, H_ = X.shape
    c = np.ascontiguousarray
    wqkv_t = c(np.asarray(W_qkv, np.float32).T.reshape(16, 128, 48, 128)
               .transpose(2, 1, 0, 3).reshape(48, 128, 2048))
    bqkv_pc = c(np.asarray(b_qkv, np.float32).reshape(48, 128).T)
    wout_t = c(np.asarray(W_out, np.float32).T.reshape(16, 128, 2048))
    bout_row = c(np.asarray(b_out, np.float32).reshape(1, 2048))
    gate_wt = c(np.asarray(gate_w, np.float32).T.reshape(16, 128, 8))
    wg = c(np.asarray(Wg, np.float32).reshape(8, 16, 128, 512)
           .transpose(0, 2, 1, 3).reshape(8, 128, 8192))
    wu = c(np.asarray(Wu, np.float32).reshape(8, 16, 128, 512)
           .transpose(0, 2, 1, 3).reshape(8, 128, 8192))
    wd = c(np.asarray(Wd, np.float32).reshape(8, 4, 128, 2048)
           .transpose(0, 2, 1, 3).reshape(8, 128, 8192))
    _swg_r = np.asarray(sWg, np.float32).reshape(16, 128, 1024).transpose(1, 0, 2)
    swg = c(np.stack([_swg_r[:, :, hh * 512:(hh + 1) * 512].reshape(128, 8192)
                      for hh in (0, 1)]))
    _swu_r = np.asarray(sWu, np.float32).reshape(16, 128, 1024).transpose(1, 0, 2)
    swu = c(np.stack([_swu_r[:, :, hh * 512:(hh + 1) * 512].reshape(128, 8192)
                      for hh in (0, 1)]))
    swd = c(np.asarray(sWd, np.float32).reshape(2, 4, 128, 2048)
            .transpose(0, 2, 1, 3).reshape(2, 128, 8192))

    ii = np.arange(128)[:, None]
    jj = np.arange(256)[None, :]
    masks = [c(np.where(jj <= ii + 128 * p, 0.0, -1e9).astype(np.float32)) for p in (0, 1)]
    maskts = [c(np.concatenate([m.T[0:128], m.T[128:256]], axis=1)) for m in masks]
    ident = np.eye(128, dtype=np.float32)
    own = [np.concatenate([np.arange(256 * j + 128 * p, 256 * j + 128 * p + 128)
                           for j in range(S_ // 256)]) for p in (0, 1)]

    in_maps = []
    for core in range(8):
        b, p = core // 2, core % 2
        in_maps.append(dict(
            x_seq=c(X[b % B_]), x_own=c(X[b % B_][own[p]]),
            wqkv_t=wqkv_t, bqkv_pc=bqkv_pc, wout_t=wout_t, bout_row=bout_row,
            gate_wt=gate_wt, wg=wg, wu=wu, wd=wd, swg=swg, swu=swu, swd=swd,
            mask=masks[p], maskt=maskts[p], ident=ident,
        ))
    return in_maps, own


def kernel(L0_attn_x, L0_attn_W_qkv, L0_attn_b_qkv, L0_attn_W_out, L0_attn_b_out,
           L0_ffn_gate_w, L0_ffn_Wg, L0_ffn_Wu, L0_ffn_Wd, L0_ffn_sWg, L0_ffn_sWu,
           L0_ffn_sWd):
    X = np.asarray(L0_attn_x, dtype=np.float32)
    in_maps, own = host_prep(X, L0_attn_W_qkv, L0_attn_b_qkv, L0_attn_W_out,
                             L0_attn_b_out, L0_ffn_gate_w, L0_ffn_Wg, L0_ffn_Wu,
                             L0_ffn_Wd, L0_ffn_sWg, L0_ffn_sWu, L0_ffn_sWd)
    nc = build_nc(S=X.shape[1])
    res = run_bass_kernel_spmd(nc, in_maps, core_ids=list(range(8)))
    Y = np.empty_like(X)
    for core in range(8):
        b, p = core // 2, core % 2
        Y[b][own[p]] = res.results[core]["out"]
    return Y


# version 22 (final)
# speedup vs baseline: 1.0684x; 1.0288x over previous
import contextlib
import math
import numpy as np
import concourse.bass as bass
import concourse.mybir as mybir
from concourse.bass_utils import run_bass_kernel_spmd

F32 = mybir.dt.float32
BF16 = mybir.dt.bfloat16
AF = mybir.ActivationFunctionType
ALU = mybir.AluOpType
AX = mybir.AxisListType

ENGINES = ("t", "v", "s", "y", "g")
DMA_QUEUES = ("y", "g", "s")


class Res:
    __slots__ = ("w", "r")

    def __init__(self):
        self.w = None
        self.r = []


class Emitter:
    def __init__(self, nc, stack, lanes=6):
        self.nc = nc
        self.q = {e: [] for e in ENGINES}
        self.esem = {e: stack.enter_context(nc.semaphore(f"dep_{e}"))
                     for e in ("t", "v", "s", "g")}
        self.ecnt = {e: 0 for e in ("t", "v", "s", "g")}
        self.lanes = lanes
        self.lsem = {}
        self.lcnt = {}
        self.lrr = {}
        for qn in DMA_QUEUES:
            self.lrr[qn] = 0
            for i in range(lanes):
                self.lsem[(qn, i)] = stack.enter_context(nc.semaphore(f"dma_{qn}{i}"))
                self.lcnt[(qn, i)] = 0
        self.known = {e: {} for e in ENGINES}
        self.fence_toks = []

    def _sem(self, key):
        return self.esem[key] if isinstance(key, str) else self.lsem[key]

    def _wait(self, eng, tok):
        key, v = tok
        if self.known[eng].get(key, 0) >= v:
            return
        sem = self._sem(key)
        self.q[eng].append(lambda E, sem=sem, v=v: E.wait_ge(sem, v))
        self.known[eng][key] = v

    def _pre(self, eng, reads, writes):
        for t in self.fence_toks:
            self._wait(eng, t)
        for res in reads:
            if res.w is not None:
                self._wait(eng, res.w)
        for res in writes:
            if res.w is not None:
                self._wait(eng, res.w)
            for t in res.r:
                self._wait(eng, t)

    def _post(self, tok, reads, writes):
        for res in reads:
            res.r.append(tok)
        for res in writes:
            res.w = tok
            res.r = []
        return tok

    def emit(self, eng, fn, reads=(), writes=()):
        self._pre(eng, reads, writes)
        sem = self.esem[eng]
        self.q[eng].append(lambda E, fn=fn, sem=sem: fn(E).then_inc(sem, 1))
        self.ecnt[eng] += 1
        return self._post((eng, self.ecnt[eng]), reads, writes)

    def emit_group(self, eng, fns, reads=(), writes=()):
        self._pre(eng, reads, writes)
        sem = self.esem[eng]
        for fn in fns[:-1]:
            self.q[eng].append(fn)
        self.q[eng].append(lambda E, fn=fns[-1], sem=sem: fn(E).then_inc(sem, 1))
        self.ecnt[eng] += 1
        return self._post((eng, self.ecnt[eng]), reads, writes)

    def dma(self, qn, fns, reads=(), writes=()):
        if callable(fns):
            fns = [fns]
        lane = self.lrr[qn]
        self.lrr[qn] = (lane + 1) % self.lanes
        key = (qn, lane)
        sem = self.lsem[key]
        prev = self.lcnt[key]
        if prev > 0:
            self._wait(qn, (key, prev))
        self._pre(qn, reads, writes)
        for fn in fns:
            self.q[qn].append(lambda E, fn=fn, sem=sem: fn(E).then_inc(sem, 16))
        self.lcnt[key] = prev + 16 * len(fns)
        return self._post((key, self.lcnt[key]), reads, writes)

    def wait_tokens(self, eng, toks):
        for t in toks:
            if t is not None:
                self._wait(eng, t)

    def fence(self):
        toks = [(e, c) for e, c in self.ecnt.items() if c > 0]
        toks += [(k, c) for k, c in self.lcnt.items() if c > 0]
        self.fence_toks = toks
        self.known = {e: dict(self.known[e]) for e in ENGINES}

    def emit_into_blocks(self, block):
        for name, attr in (("g", "gpsimd"), ("y", "sync"), ("t", "tensor"),
                           ("v", "vector"), ("s", "scalar")):
            closures = self.q[name]

            def run(E, closures=closures):
                for c in closures:
                    c(E)
            getattr(block, attr)(run)


H = 2048
NH = 16
HD = 128
IC = 16
NEXP = 8
M = 512
MS = 1024
SCL = 1.0 / math.sqrt(HD)


def build_nc(S=4096, debug=False):
    OWN = S // 2
    nQT = OWN // 128
    nTT = S // 128
    nKT = S // 128
    nc = bass.Bass("TRN2", target_bir_lowering=False, debug=False)

    d = lambda n, shp, dt, kind: nc.dram_tensor(n, shp, dt, kind=kind).ap()
    x_seq = d("x_seq", [S, H], F32, "ExternalInput")
    x_own = d("x_own", [OWN, H], F32, "ExternalInput")
    wqkv_t = d("wqkv_t", [48, 128, 2048], F32, "ExternalInput")
    bqkv_pc = d("bqkv_pc", [128, 48], F32, "ExternalInput")
    wout_t = d("wout_t", [16, 128, H], F32, "ExternalInput")
    bout_row = d("bout_row", [1, H], F32, "ExternalInput")
    gate_wt = d("gate_wt", [16, 128, NEXP], F32, "ExternalInput")
    wg = d("wg", [NEXP, 128, 8192], F32, "ExternalInput")
    wu = d("wu", [NEXP, 128, 8192], F32, "ExternalInput")
    wd = d("wd", [NEXP, 128, 8192], F32, "ExternalInput")
    swg = d("swg", [2, 128, 8192], F32, "ExternalInput")
    swu = d("swu", [2, 128, 8192], F32, "ExternalInput")
    swd = d("swd", [2, 128, 8192], F32, "ExternalInput")
    mask_in = d("mask", [128, 256], F32, "ExternalInput")
    maskt_in = d("maskt", [128, 256], F32, "ExternalInput")
    ident_in = d("ident", [128, 128], F32, "ExternalInput")
    out = d("out", [OWN, H], F32, "ExternalOutput")

    SK = "ExternalOutput" if debug else "Internal"
    kf_s = d("kf_s", [NH, 128, S], BF16, SK)
    vf_s = d("vf_s", [NH, 128, S], BF16, SK)
    qf_s = d("qf_s", [NH, 128, OWN], BF16, SK)
    h_s = d("h_s", [nQT, 128, H], F32, SK)
    hn_s = d("hn_s", [nQT, 128, H], BF16, SK)
    attn_dbg = d("attn_dbg", [128, NH * (S // 2)], BF16, SK) if debug else None
    p_dbg = d("p_dbg", [4, 128, S], BF16, SK) if debug else None
    vt_dbg = d("vt_dbg", [128, (S // 128) * 144], BF16, SK) if debug else None
    av_dbg = d("av_dbg", [2, 128, 129], F32, SK) if debug else None

    with contextlib.ExitStack() as top:
        E_ = top.enter_context
        sb = lambda n, shp, dt: E_(nc.sbuf_tensor(n, shp, dt))
        em = Emitter(nc, top)

        psA = E_(nc.psum_tensor("psA", [128, 1024], F32))
        psB = E_(nc.psum_tensor("psB", [128, 1024], F32))
        psC = E_(nc.psum_tensor("psC", [128, 512], F32))
        psD = E_(nc.psum_tensor("psD", [128, 512], F32))
        psT2 = E_(nc.psum_tensor("psT2", [128, 1024], BF16))
        psT = E_(nc.psum_tensor("psT", [128, 1024], BF16))

        bqkv_sb = sb("bqkv_sb", [128, 48], F32)
        sbq_sb = sb("sbq_sb", [128, 16], F32)
        mask_sb = sb("mask_sb", [128, 256], F32)
        eps_sb = sb("eps_sb", [128, 1], F32)
        ones1 = sb("ones1", [1, 128], BF16)
        r_bqkv, r_sbq, r_mask, r_eps = Res(), Res(), Res(), Res()
        r_ones = Res()
        em.dma("y", lambda E: E.dma_start(out=bqkv_sb[:], in_=bqkv_pc), writes=[r_bqkv])
        em.dma("y", lambda E: E.dma_start(out=mask_sb[:], in_=mask_in), writes=[r_mask])
        em.emit("v", lambda E: E.memset(eps_sb[:], 1e-5), writes=[r_eps])
        em.emit("v", lambda E: E.memset(ones1[:], 1.0), writes=[r_ones])
        em.emit("v", lambda E: E.tensor_scalar_mul(sbq_sb[:], bqkv_sb[:, 0:16], SCL),
                reads=[r_bqkv], writes=[r_sbq])
        ident_bf = sb("ident_bf", [128, 128], BF16)
        ident512 = sb("ident512", [128, 512], BF16)
        maskt_sb = sb("maskt_sb", [128, 256], F32)
        r_ident, r_id512, r_maskt = Res(), Res(), Res()
        em.dma("g", lambda E: E.dma_start(out=ident_bf[:], in_=ident_in), writes=[r_ident])
        em.dma("y", lambda E: E.dma_start(out=maskt_sb[:], in_=maskt_in), writes=[r_maskt])
        for j4 in range(4):
            em.emit("v", lambda E, j4=j4: E.tensor_copy(ident512[:, j4 * 128:(j4 + 1) * 128],
                                                        ident_bf[:]),
                    reads=[r_ident], writes=[r_id512])

        r_psT = Res()
        r_psT2 = Res()
        tp_rr = [0]

        def emit_tpose(srcs, dsts, reads, writes, batch_dsts=None):
            n = len(srcs)
            for bi, b0 in enumerate(range(0, n, 8)):
                blk = list(range(b0, min(b0 + 8, n)))
                k = len(blk)
                tb = psT if tp_rr[0] % 2 == 0 else psT2
                r_tb = r_psT if tp_rr[0] % 2 == 0 else r_psT2
                tp_rr[0] += 1
                em.emit_group("t", [
                    (lambda E, j=j, i=i, tb=tb: E.transpose(tb[:, i * 128:(i + 1) * 128], srcs[j],
                                                            ident_bf[:]))
                    for i, j in enumerate(blk)
                ], reads=list(reads) + [r_ident], writes=[r_tb])
                if batch_dsts is not None:
                    em.emit("s", lambda E, bi=bi, k=k, tb=tb: E.activation(
                        batch_dsts[bi],
                        tb[:, 0:k * 128].rearrange("p (i s) -> p i s", i=k),
                        AF.Identity),
                        reads=[r_tb], writes=list(writes))
                else:
                    for i, j in enumerate(blk):
                        em.emit("s", lambda E, j=j, i=i, tb=tb: E.activation(
                            dsts[j], tb[:, i * 128:(i + 1) * 128], AF.Identity),
                            reads=[r_tb], writes=list(writes))

        def emit_ln(xa, r_xa, xn, r_xn, smt, r_sm, bnb, r_bnb):
            for i in range(4):
                em.emit("v", lambda E, i=i: E.bn_stats(bnb[:, i * 6:(i + 1) * 6],
                                                       xa[:, i * 512:(i + 1) * 512]),
                        reads=[r_xa], writes=[r_bnb])
            em.emit("v", lambda E: E.bn_aggr(smt[:, 0:2], bnb[:]), reads=[r_bnb], writes=[r_sm])
            em.emit("s", lambda E: E.activation(smt[:, 2:3], smt[:, 1:2], AF.Ln,
                                                bias=eps_sb[:], scale=1.0),
                    reads=[r_sm, r_eps], writes=[r_sm])
            em.emit("s", lambda E: E.activation(smt[:, 3:4], smt[:, 2:3], AF.Exp,
                                                bias=0.0, scale=-0.5),
                    reads=[r_sm], writes=[r_sm])
            em.emit("v", lambda E: E.tensor_scalar(xn[:], xa[:], smt[:, 0:1], smt[:, 3:4],
                                                   ALU.subtract, ALU.mult),
                    reads=[r_xa, r_sm], writes=[r_xn])

        p12 = contextlib.ExitStack()
        sb12 = lambda n, shp, dt: p12.enter_context(nc.sbuf_tensor(n, shp, dt))
        xnow = sb12("xnow", [128, IC * OWN], BF16)
        xt_12 = [sb12(f"xt12_{k}", [128, H], F32) for k in range(2)]
        xn_12 = [sb12(f"xn12_{k}", [128, H], BF16) for k in range(2)]
        sm_12 = [sb12(f"sm12_{k}", [128, 8], F32) for k in range(2)]
        bnb_12 = [sb12(f"bnb12_{k}", [128, 24], F32) for k in range(2)]

        r_xt = [Res(), Res()]
        r_xn = [Res(), Res()]
        r_sm = [Res(), Res()]
        r_bnb = [Res(), Res()]
        r_xnow = [Res() for _ in range(nQT)]
        r_qfs = [Res() for _ in range(NH)]
        r_kfs = [Res() for _ in range(NH)]
        r_vfs = [Res() for _ in range(NH)]

        def ln_tile12(t):
            k = t % 2
            em.dma("y", lambda E, t=t, k=k: E.dma_start(out=xt_12[k][:],
                                                        in_=x_own[t * 128:(t + 1) * 128, :]),
                   writes=[r_xt[k]])
            emit_ln(xt_12[k], r_xt[k], xn_12[k], r_xn[k], sm_12[k], r_sm[k], bnb_12[k], r_bnb[k])
            xv = xnow[:].rearrange("p (i s) -> p i s", i=IC)
            emit_tpose([xn_12[k][:, i * 128:(i + 1) * 128] for i in range(IC)], None,
                       [r_xn[k]], [r_xnow[t]],
                       batch_dsts=[xv[:, 0:8, t * 128:(t + 1) * 128],
                                   xv[:, 8:16, t * 128:(t + 1) * 128]])

        for t in range(4):
            ln_tile12(t)
        nOQ = OWN // 512
        ps4 = [psA[:, 0:512], psA[:, 512:1024], psB[:, 0:512], psB[:, 512:1024]]
        r_ps4 = [Res() for _ in range(4)]
        qst = [sb12(f"qst{j}", [128, 512], BF16) for j in range(4)]
        r_qst = [Res() for _ in range(4)]
        wq8 = [sb12(f"wq8_{j}", [128, 2048], BF16) for j in range(16)]
        r_wq8 = [Res() for _ in range(16)]
        for o in range(16):
            em.dma("g", lambda E, o=o: E.dma_start(out=wq8[o][:], in_=wqkv_t[o]),
                   writes=[r_wq8[o]])
        for n in range(nOQ):
            for t in range(4 + 4 * (n - 1), 4 + 4 * n):
                if 0 <= t < nQT:
                    ln_tile12(t)
            rdt = [r_xnow[tt] for tt in range(n * 4, n * 4 + 4)]
            for o in range(16):
                pj = (n * 16 + o) % 4
                em.emit_group("t", [
                    (lambda E, i=i, o=o, n=n, pj=pj: E.matmul(
                        ps4[pj], wq8[o][:, i * 128:(i + 1) * 128],
                        xnow[:, i * OWN + n * 512: i * OWN + (n + 1) * 512],
                        start=(i == 0), stop=(i == IC - 1)))
                    for i in range(IC)
                ], reads=[r_wq8[o]] + rdt, writes=[r_ps4[pj]])
                sj = (n * 16 + o) % 4
                em.emit("s", lambda E, o=o, pj=pj, sj=sj: E.activation(
                    qst[sj][:], ps4[pj], AF.Identity,
                    bias=sbq_sb[:, o:o + 1], scale=SCL),
                    reads=[r_ps4[pj], r_sbq], writes=[r_qst[sj]])
                em.dma("y", lambda E, o=o, n=n, sj=sj: E.dma_start(
                    out=qf_s[o][:, n * 512:(n + 1) * 512], in_=qst[sj][:]),
                    reads=[r_qst[sj]], writes=[r_qfs[o]])

        em.fence()
        p12.close()

        p34 = contextlib.ExitStack()
        sb34 = lambda n, shp, dt: p34.enter_context(nc.sbuf_tensor(n, shp, dt))
        xnf_r = sb34("xnf_r", [128, IC * 1024], BF16)
        xt_34 = [sb34(f"xt34_{k}", [128, H], F32) for k in range(2)]
        xn_34 = [sb34(f"xn34_{k}", [128, H], BF16) for k in range(2)]
        sm_34 = [sb34(f"sm34_{k}", [128, 8], F32) for k in range(2)]
        bnb_34 = [sb34(f"bnb34_{k}", [128, 24], F32) for k in range(2)]

        r_xt34 = [Res(), Res()]
        r_xn34 = [Res(), Res()]
        r_sm34 = [Res(), Res()]
        r_bnb34 = [Res(), Res()]
        r_xnfr = [Res() for _ in range(8)]

        def ln_tile34(t):
            k = t % 2
            em.dma("y", lambda E, t=t, k=k: E.dma_start(out=xt_34[k][:],
                                                        in_=x_seq[t * 128:(t + 1) * 128, :]),
                   writes=[r_xt34[k]])
            emit_ln(xt_34[k], r_xt34[k], xn_34[k], r_xn34[k], sm_34[k], r_sm34[k],
                    bnb_34[k], r_bnb34[k])
            xfv = xnf_r[:].rearrange("p (i s) -> p i s", i=IC)
            ts = t % 8
            emit_tpose([xn_34[k][:, i * 128:(i + 1) * 128] for i in range(IC)], None,
                       [r_xn34[k]], [r_xnfr[ts]],
                       batch_dsts=[xfv[:, 0:8, ts * 128:(ts + 1) * 128],
                                   xfv[:, 8:16, ts * 128:(ts + 1) * 128]])

        for t in range(8):
            ln_tile34(t)

        nSQ = S // 512
        kst = [sb34(f"kst{j}", [128, 512], BF16) for j in range(4)]
        r_kst = [Res() for _ in range(4)]
        wkv = [sb34(f"wkv{o}", [128, 2048], BF16) for o in range(32)]
        r_wkv = [Res() for _ in range(32)]
        for o in range(16, 48):
            em.dma("g", lambda E, o=o: E.dma_start(out=wkv[o - 16][:], in_=wqkv_t[o]),
                   writes=[r_wkv[o - 16]])
        for n in range(nSQ):
            rdt = [r_xnfr[tt % 8] for tt in range(n * 4, n * 4 + 4)]
            c0 = (n % 2) * 512
            for oo in range(32):
                o = 16 + oo
                pj = (n * 32 + oo) % 4
                em.emit_group("t", [
                    (lambda E, i=i, oo=oo, c0=c0, pj=pj: E.matmul(
                        ps4[pj], wkv[oo][:, i * 128:(i + 1) * 128],
                        xnf_r[:, i * 1024 + c0: i * 1024 + c0 + 512],
                        start=(i == 0), stop=(i == IC - 1)))
                    for i in range(IC)
                ], reads=[r_wkv[oo]] + rdt, writes=[r_ps4[pj]])
                sj = (n * 32 + oo) % 4
                em.emit("s", lambda E, o=o, pj=pj, sj=sj: E.activation(
                    kst[sj][:], ps4[pj], AF.Identity,
                    bias=bqkv_sb[:, o:o + 1], scale=1.0),
                    reads=[r_ps4[pj], r_bqkv], writes=[r_kst[sj]])
                if o < 32:
                    em.dma("y", lambda E, o=o, n=n, sj=sj: E.dma_start(
                        out=kf_s[o - 16][:, n * 512:(n + 1) * 512], in_=kst[sj][:]),
                        reads=[r_kst[sj]], writes=[r_kfs[o - 16]])
                else:
                    em.dma("y", lambda E, o=o, n=n, sj=sj: E.dma_start(
                        out=vf_s[o - 32][:, n * 512:(n + 1) * 512], in_=kst[sj][:]),
                        reads=[r_kst[sj]], writes=[r_vfs[o - 32]])
            for t in range(8 + 4 * n, min(8 + 4 * n + 4, nTT)):
                ln_tile34(t)

        em.fence()
        p34.close()

        p56 = contextlib.ExitStack()
        attn_fm = p56.enter_context(nc.sbuf_tensor("attn_fm", [128, NH * OWN], BF16))
        p5 = contextlib.ExitStack()
        sb5 = lambda n, shp, dt: p5.enter_context(nc.sbuf_tensor(n, shp, dt))
        kf_sb = [sb5(f"kf_{k}", [128, S], BF16) for k in range(2)]
        qf_sb = [sb5(f"qf_{k}", [128, OWN], BF16) for k in range(2)]
        vfm_sb = [sb5(f"vfm_{k}", [128, S], BF16) for k in range(2)]
        vt_sb = [sb5(f"vt_{k}", [128, nKT * 129], BF16) for k in range(2)]
        pt_sb = [sb5(f"pt_{k}", [128, nKT * 512], BF16) for k in range(2)]
        at_sb = [sb5(f"at_{k}", [128, 128], BF16) for k in range(2)]
        sm5 = [sb5(f"sm5_{k}", [128, 8], F32) for k in range(2)]
        cmax = [sb5(f"cmax_{k}", [128, 40], F32) for k in range(2)]
        cmbf = [sb5(f"cmbf_{k}", [128, 1], BF16) for k in range(2)]
        mg_bf = [sb5(f"mg_{k}", [1, 1], BF16) for k in range(2)]
        bias_ap = [sb5(f"biasap_{k}", [128, 1], F32) for k in range(2)]

        r_kf = [Res(), Res()]
        r_qf = [Res(), Res()]
        r_vfm = [Res(), Res()]
        r_vt = [Res(), Res()]
        r_pt = [Res(), Res()]
        r_at = [Res(), Res()]
        r_sm5 = [Res(), Res()]
        r_cmax = [Res(), Res()]
        r_cmbf = [Res(), Res()]
        r_mg = [Res(), Res()]
        r_bias = [Res(), Res()]
        r_st4 = [Res() for _ in range(4)]
        r_av = [Res(), Res()]
        r_afm = {}
        st4 = [psA[:, 0:512], psA[:, 512:1024], psB[:, 0:512], psB[:, 512:1024]]

        for k in range(2):
            em.emit_group("v", [
                (lambda E, k=k, kt=kt: E.memset(vt_sb[k][:, kt * 129 + 128: kt * 129 + 129], 1.0))
                for kt in range(nKT)
            ], writes=[r_vt[k]])

        def load_head(h):
            hb = h % 2
            em.dma("y", lambda E, h=h, hb=hb: E.dma_start(out=kf_sb[hb][:], in_=kf_s[h]),
                   reads=[r_kfs[h]], writes=[r_kf[hb]])
            em.dma("y", lambda E, h=h, hb=hb: E.dma_start(out=qf_sb[hb][:], in_=qf_s[h]),
                   reads=[r_qfs[h]], writes=[r_qf[hb]])
            em.dma("g", lambda E, h=h, hb=hb: E.dma_start(out=vfm_sb[hb][:], in_=vf_s[h]),
                   reads=[r_vfs[h]], writes=[r_vfm[hb]])
            vv = vt_sb[hb][:].rearrange("p (k c) -> p k c", c=129)
            emit_tpose([vfm_sb[hb][:, kt * 128:(kt + 1) * 128] for kt in range(nKT)], None,
                       [r_vfm[hb]], [r_vt[hb]],
                       batch_dsts=[vv[:, b0:min(b0 + 8, nKT), 0:128]
                                   for b0 in range(0, nKT, 8)])

        load_head(0)
        load_head(1)

        nQG = nQT // 4
        slot_ctr = [0]

        def st_block(h, g, gb, kt, do_stats):
            hb = h % 2
            jmin = max(0, kt // 2 - 4 * g)
            N = 512 - jmin * 128
            sj = slot_ctr[0] % 4
            slot_ctr[0] += 1
            stb = st4[sj]
            r_st = r_st4[sj]
            em.emit("t", lambda E, kt=kt, g=g, hb=hb, jmin=jmin, N=N, stb=stb: E.matmul(
                stb[:, 0:N], kf_sb[hb][:, kt * 128:(kt + 1) * 128],
                qf_sb[hb][:, (4 * g + jmin) * 128:(4 * g + 4) * 128],
                start=True, stop=True),
                reads=[r_kf[hb], r_qf[hb]], writes=[r_st])
            if do_stats:
                em.emit("v", lambda E, N=N, kt=kt, stb=stb, gb=gb: E.reduce_max(
                    cmax[gb][:, kt:kt + 1], stb[:, 0:N], axis=AX.X),
                    reads=[r_st], writes=[r_cmax[gb]])
            else:
                if kt >= 8 * g:
                    mh = 0 if kt % 2 == 0 else 1
                    em.emit("v", lambda E, stb=stb, mh=mh: E.tensor_add(
                        stb[:, 0:128], stb[:, 0:128], maskt_sb[:, mh * 128:(mh + 1) * 128]),
                        reads=[r_maskt], writes=[r_st])
                em.emit("s", lambda E, kt=kt, N=N, stb=stb, gb=gb: E.activation(
                    pt_sb[gb][:, kt * 512: kt * 512 + N], stb[:, 0:N], AF.Exp,
                    bias=bias_ap[gb][:], scale=1.0),
                    reads=[r_st, r_bias[gb]], writes=[r_pt[gb]])

        def st_mms(h, g, gb, do_stats, inter=None):
            nktm = 8 * g + 8
            pk = 0
            pn = 8 * inter[1] + 8 if inter is not None else 0
            for kt in range(nktm):
                st_block(h, g, gb, kt, do_stats)
                if inter is not None and pk < pn:
                    st_block(inter[0], inter[1], inter[2], pk, False)
                    pk += 1
            while inter is not None and pk < pn:
                st_block(inter[0], inter[1], inter[2], pk, False)
                pk += 1

        def m_chain(h, g, gb):
            nktm = 8 * g + 8
            em.emit("v", lambda E, gb=gb, nktm=nktm: E.reduce_max(
                sm5[gb][:, 0:1], cmax[gb][:, 0:nktm], axis=AX.X),
                reads=[r_cmax[gb]], writes=[r_sm5[gb]])
            em.emit("v", lambda E, gb=gb: E.tensor_copy(cmbf[gb][:], sm5[gb][:, 0:1]),
                    reads=[r_sm5[gb]], writes=[r_cmbf[gb]])
            em.emit("t", lambda E, gb=gb: E.matmul(psC[0:1, 256:384], cmbf[gb][:], ident_bf[:],
                                                   start=True, stop=True),
                    reads=[r_cmbf[gb], r_ident], writes=[r_av[0]])
            em.emit("v", lambda E, gb=gb: E.reduce_max(sm5[gb][0:1, 2:3], psC[0:1, 256:384], axis=AX.X),
                    reads=[r_av[0]], writes=[r_sm5[gb]])
            em.emit("v", lambda E, gb=gb: E.tensor_copy(mg_bf[gb][:], sm5[gb][0:1, 2:3]),
                    reads=[r_sm5[gb]], writes=[r_mg[gb]])
            em.emit("t", lambda E, gb=gb: E.matmul(psC[:, 384:385], ones1[:], mg_bf[gb][:],
                                                   start=True, stop=True),
                    reads=[r_ones, r_mg[gb]], writes=[r_av[0]])
            em.emit("v", lambda E, gb=gb: E.tensor_scalar_mul(bias_ap[gb][:], psC[:, 384:385], -1.0),
                    reads=[r_av[0]], writes=[r_bias[gb]])

        def av_evac(h, g, gb, j):
            qt = 4 * g + j
            ab = j % 2
            av = psC[:, 0:129] if ab == 0 else psD[:, 0:129]
            r_a = r_av[ab]
            em.emit("v", lambda E, gb=gb, av=av, ab=ab: E.tensor_scalar_max(
                sm5[gb][:, 4 + ab:5 + ab], av[:, 128:129], 1e-35),
                reads=[r_a], writes=[r_sm5[gb]])
            em.emit("v", lambda E, gb=gb, ab=ab: E.reciprocal(
                sm5[gb][:, 6 + ab:7 + ab], sm5[gb][:, 4 + ab:5 + ab]),
                reads=[r_sm5[gb]], writes=[r_sm5[gb]])
            em.emit("v", lambda E, gb=gb, av=av, ab=ab: E.tensor_scalar_mul(
                at_sb[ab][:], av[:, 0:128], sm5[gb][:, 6 + ab:7 + ab]),
                reads=[r_a, r_sm5[gb]], writes=[r_at[ab]])
            r_afm[(h, qt)] = Res()
            emit_tpose([at_sb[ab][:]],
                       [attn_fm[:, h * OWN + qt * 128: h * OWN + (qt + 1) * 128]],
                       [r_at[ab]], [r_afm[(h, qt)]])

        def av_group(h, g, gb):
            hb = h % 2
            for j in range(4):
                qt = 4 * g + j
                nkt = 2 * (qt + 1)
                ab = j % 2
                av = psC[:, 0:129] if ab == 0 else psD[:, 0:129]
                r_a = r_av[ab]
                fns = []
                for kt in range(nkt):
                    jmin = max(0, kt // 2 - 4 * g)
                    jr = j - jmin
                    fns.append(lambda E, kt=kt, jr=jr, gb=gb, hb=hb, av=av, nkt=nkt: E.matmul(
                        av, pt_sb[gb][:, kt * 512 + jr * 128: kt * 512 + (jr + 1) * 128],
                        vt_sb[hb][:, kt * 129: kt * 129 + 129],
                        start=(kt == 0), stop=(kt == nkt - 1)))
                em.emit_group("t", fns, reads=[r_pt[gb], r_vt[hb]], writes=[r_a])
                if j > 0:
                    av_evac(h, g, gb, j - 1)
            av_evac(h, g, gb, 3)

        prev = None
        for h in range(NH):
            for g in range(nQG):
                gb = (h * nQG + g) % 2
                st_mms(h, g, gb, do_stats=True, inter=prev)
                if prev is not None:
                    av_group(prev[0], prev[1], prev[2])
                m_chain(h, g, gb)
                prev = (h, g, gb)
            ph, pg, pgb = prev
            st_mms(ph, pg, pgb, do_stats=False)
            av_group(ph, pg, pgb)
            prev = None
            if h + 2 < NH:
                load_head(h + 2)

        if debug:
            r_all = [r_afm[(hh, qq)] for hh in range(NH) for qq in range(nQT)]
            em.dma("y", lambda E: E.dma_start(out=attn_dbg, in_=attn_fm[:]), reads=r_all)

        em.fence()
        p5.close()

        p6 = contextlib.ExitStack()
        sb6 = lambda n, shp, dt: p6.enter_context(nc.sbuf_tensor(n, shp, dt))
        wout_sb = sb6("wout_sb", [128, 16 * H], BF16)
        bout_f = sb6("bout_f", [1, H], F32)
        bout_bf = sb6("bout_bf", [1, H], BF16)
        r_boutf, r_boutbf = Res(), Res()
        em.dma("y", lambda E: E.dma_start(out=bout_f[:], in_=bout_row), writes=[r_boutf])
        em.emit("v", lambda E: E.tensor_copy(bout_bf[:], bout_f[:]),
                reads=[r_boutf], writes=[r_boutbf])
        xt_6 = [sb6(f"xt6_{k}", [128, H], F32) for k in range(2)]
        ht_6 = [sb6(f"ht6_{k}", [128, H], F32) for k in range(2)]
        hn_6 = [sb6(f"hn6_{k}", [128, H], BF16) for k in range(2)]
        sm_6 = [sb6(f"sm6_{k}", [128, 8], F32) for k in range(2)]
        bnb_6 = [sb6(f"bnb6_{k}", [128, 24], F32) for k in range(2)]

        r_wout = [Res() for _ in range(16)]
        r_xt6 = [Res(), Res()]
        r_ht6 = [Res(), Res()]
        r_hn6 = [Res(), Res()]
        r_sm6 = [Res(), Res()]
        r_bnb6 = [Res(), Res()]
        r_hs = [Res() for _ in range(nQT)]
        r_hns = [Res() for _ in range(nQT)]

        for i in range(16):
            em.dma("g", lambda E, i=i: E.dma_start(out=wout_sb[:, i * H:(i + 1) * H], in_=wout_t[i]),
                   writes=[r_wout[i]])

        for qt in range(nQT):
            k = qt % 2
            em.dma("y", lambda E, qt=qt, k=k: E.dma_start(out=xt_6[k][:],
                                                          in_=x_own[qt * 128:(qt + 1) * 128, :]),
                   writes=[r_xt6[k]])
            for oc in range(4):
                pj = (qt * 4 + oc) % 4
                fns = [lambda E, oc=oc, pj=pj: E.matmul(
                    ps4[pj], ones1[:], bout_bf[0:1, oc * 512:(oc + 1) * 512],
                    start=True, stop=False)]
                fns += [
                    (lambda E, i=i, qt=qt, oc=oc, pj=pj: E.matmul(
                        ps4[pj], attn_fm[:, i * OWN + qt * 128: i * OWN + (qt + 1) * 128],
                        wout_sb[:, i * H + oc * 512: i * H + oc * 512 + 512],
                        start=False, stop=(i == 15)))
                    for i in range(16)
                ]
                em.emit_group("t", fns,
                              reads=[r_ones, r_boutbf] +
                                    [r_afm[(i, qt)] for i in range(NH)] + r_wout,
                              writes=[r_ps4[pj]])
                em.emit("v", lambda E, oc=oc, k=k, pj=pj: E.tensor_add(
                    ht_6[k][:, oc * 512:(oc + 1) * 512], ps4[pj], xt_6[k][:, oc * 512:(oc + 1) * 512]),
                    reads=[r_ps4[pj], r_xt6[k]], writes=[r_ht6[k]])
            em.dma("y", lambda E, qt=qt, k=k: E.dma_start(out=h_s[qt], in_=ht_6[k][:]),
                   reads=[r_ht6[k]], writes=[r_hs[qt]])
            emit_ln(ht_6[k], r_ht6[k], hn_6[k], r_hn6[k], sm_6[k], r_sm6[k], bnb_6[k], r_bnb6[k])
            em.dma("y", lambda E, qt=qt, k=k: E.dma_start(out=hn_s[qt], in_=hn_6[k][:]),
                   reads=[r_hn6[k]], writes=[r_hns[qt]])

        em.fence()
        p6.close()
        p56.close()

        p7 = contextlib.ExitStack()
        sb7 = lambda n, shp, dt: p7.enter_context(nc.sbuf_tensor(n, shp, dt))
        TPP = nQT // 2
        hnf = sb7("hnf", [128, IC * TPP * 128], BF16)
        y_sb = sb7("y_sb", [128, TPP * H], F32)
        slot = [sb7(f"slot{j}", [128, 8192], BF16) for j in range(4)]
        gate_bf = sb7("gate_bf", [128, 128], BF16)
        comb = sb7("comb", [128, TPP * 8], F32)
        lgt = sb7("lgt", [128, 16], F32)
        sm7 = sb7("sm7", [128, 8], F32)
        sil_7 = [sb7(f"sil7_{k}", [128, 512], F32) for k in range(2)]
        td_7 = [sb7(f"td7_{k}", [128, 512], BF16) for k in range(2)]
        tdf_7 = [sb7(f"tdf7_{k}", [128, 512], BF16) for k in range(2)]
        tdfs = [sb7(f"tdfs_{k}", [128, TPP * 512], BF16) for k in range(2)]
        ht_7 = [sb7("ht7_0", [128, H], F32)]
        hnt_7 = [sb7(f"hnt7_{k}", [128, H], BF16) for k in range(2)]
        r_hnt7 = [Res(), Res()]

        r_slot = [Res() for _ in range(4)]
        r_gate = Res()
        r_comb = Res()
        r_lgt = Res()
        r_sm7 = Res()
        r_sil = [Res(), Res()]
        r_td = [Res(), Res()]
        r_tdf = [Res(), Res()]
        r_tdfs = [Res(), Res()]
        r_hnf = [Res() for _ in range(TPP)]
        r_ht7 = [Res()]
        r_yfin = [Res() for _ in range(TPP)]
        r_y = Res()
        st_toks = []

        em.dma("g", [
            (lambda E, i=i: E.dma_start(out=gate_bf[:, i * 8:(i + 1) * 8], in_=gate_wt[i]))
            for i in range(16)
        ], writes=[r_gate])

        slot_rr = [0]

        def load_slot(fns, rds=()):
            j = slot_rr[0]
            slot_rr[0] = (j + 1) % 4
            em.dma("g", [
                (lambda E, fn=fn, j=j: fn(E, slot[j])) for fn in fns
            ], reads=list(rds), writes=[r_slot[j]])
            return j

        def wg_loader(src, e):
            return [(lambda E, S_, src=src, e=e: E.dma_start(out=S_[:], in_=src[e]))]

        def wd_loader(e):
            return [(lambda E, S_, e=e: E.dma_start(out=S_[:], in_=wd[e]))]

        def sw_loader(src, half):
            return [(lambda E, S_, src=src, half=half: E.dma_start(out=S_[:], in_=src[half]))]

        def swd_loader(half):
            return [(lambda E, S_, half=half: E.dma_start(out=S_[:], in_=swd[half]))]

        y_ps = [psA[:, 0:512], psA[:, 512:1024]]
        r_yps = [Res(), Res()]
        g_ps2 = [psC, psC]
        u_ps2 = [psD, psB[:, 512:1024]]
        r_gc = Res()
        r_gps2 = [r_gc, r_gc]
        r_ups2 = [Res(), Res()]
        lg_ps = psB[:, 0:512]
        r_lgps = Res()

        for pa in range(2):
            t0 = pa * TPP
            for t in range(TPP):
                kb = t % 2
                em.dma("y", lambda E, t=t, pa=pa, kb=kb: E.dma_start(
                    out=hnt_7[kb][:], in_=hn_s[pa * TPP + t]),
                    reads=[r_hns[pa * TPP + t]], writes=[r_hnt7[kb]])
                hv = hnf[:].rearrange("p (i s) -> p i s", i=IC)
                emit_tpose([hnt_7[kb][:, i * 128:(i + 1) * 128] for i in range(IC)], None,
                           [r_hnt7[kb]], [r_hnf[t]],
                           batch_dsts=[hv[:, 0:8, t * 128:(t + 1) * 128],
                                       hv[:, 8:16, t * 128:(t + 1) * 128]])
            em.emit_group("v", [
                (lambda E, t=t: E.memset(y_sb[:, t * H:(t + 1) * H], 0.0)) for t in range(TPP)
            ], writes=[r_y])

            for t in range(TPP):
                em.emit_group("t", [
                    (lambda E, i=i, t=t: E.matmul(
                        lg_ps[:, 0:8], hnf[:, i * TPP * 128 + t * 128: i * TPP * 128 + (t + 1) * 128],
                        gate_bf[:, i * 8:(i + 1) * 8], start=(i == 0), stop=(i == IC - 1)))
                    for i in range(IC)
                ], reads=[r_gate, r_hnf[t]], writes=[r_lgps])
                em.emit("v", lambda E: E.tensor_copy(lgt[:, 0:8], lg_ps[:, 0:8]),
                        reads=[r_lgps], writes=[r_lgt])
                em.emit("v", lambda E: E.reduce_max(sm7[:, 0:1], lgt[:, 0:8], axis=AX.X),
                        reads=[r_lgt], writes=[r_sm7])
                em.emit("v", lambda E: E.tensor_scalar(lgt[:, 8:16], lgt[:, 0:8], sm7[:, 0:1],
                                                       None, ALU.is_equal),
                        reads=[r_lgt, r_sm7], writes=[r_lgt])
                em.emit("v", lambda E, t=t: E.scalar_tensor_tensor(
                    comb[:, t * 8:(t + 1) * 8], lgt[:, 8:16], -1e30, lgt[:, 0:8],
                    ALU.mult, ALU.add), reads=[r_lgt], writes=[r_comb])
                em.emit("v", lambda E, t=t: E.reduce_max(sm7[:, 1:2], comb[:, t * 8:(t + 1) * 8],
                                                         axis=AX.X),
                        reads=[r_comb], writes=[r_sm7])
                em.emit("v", lambda E: E.tensor_sub(sm7[:, 2:3], sm7[:, 1:2], sm7[:, 0:1]),
                        reads=[r_sm7], writes=[r_sm7])
                em.emit("s", lambda E: E.activation(sm7[:, 3:4], sm7[:, 2:3], AF.Exp,
                                                    bias=0.0, scale=1.0),
                        reads=[r_sm7], writes=[r_sm7])
                em.emit("v", lambda E: E.tensor_scalar(sm7[:, 4:5], sm7[:, 3:4], 1.0, None, ALU.add),
                        reads=[r_sm7], writes=[r_sm7])
                em.emit("v", lambda E: E.reciprocal(sm7[:, 5:6], sm7[:, 4:5]),
                        reads=[r_sm7], writes=[r_sm7])
                em.emit("v", lambda E: E.tensor_mul(sm7[:, 6:7], sm7[:, 3:4], sm7[:, 5:6]),
                        reads=[r_sm7], writes=[r_sm7])
                em.emit("v", lambda E, t=t: E.tensor_scalar(
                    comb[:, t * 8:(t + 1) * 8], lgt[:, 0:8], sm7[:, 1:2], sm7[:, 6:7],
                    ALU.is_equal, ALU.mult), reads=[r_lgt, r_sm7], writes=[r_comb])
                em.emit("v", lambda E, t=t: E.tensor_scalar(
                    lgt[:, 8:16], lgt[:, 0:8], sm7[:, 0:1], sm7[:, 5:6],
                    ALU.is_equal, ALU.mult), reads=[r_lgt, r_sm7], writes=[r_lgt])
                em.emit("v", lambda E, t=t: E.tensor_add(
                    comb[:, t * 8:(t + 1) * 8], comb[:, t * 8:(t + 1) * 8], lgt[:, 8:16]),
                    reads=[r_lgt], writes=[r_comb])

            def gut_part(t, e, jg, ju):
                kb = t % 2
                g_ps, u_ps = g_ps2[kb], u_ps2[kb]
                r_gps, r_ups = r_gps2[kb], r_ups2[kb]
                em.emit_group("t", [
                    (lambda E, i=i, t=t, jg=jg: E.matmul(
                        g_ps[:], hnf[:, i * TPP * 128 + t * 128: i * TPP * 128 + (t + 1) * 128],
                        slot[jg][:, i * 512:(i + 1) * 512], start=(i == 0), stop=(i == IC - 1)))
                    for i in range(IC)
                ], reads=[r_slot[jg], r_hnf[t]], writes=[r_gps])
                em.emit_group("t", [
                    (lambda E, i=i, t=t, ju=ju: E.matmul(
                        u_ps[:], hnf[:, i * TPP * 128 + t * 128: i * TPP * 128 + (t + 1) * 128],
                        slot[ju][:, i * 512:(i + 1) * 512], start=(i == 0), stop=(i == IC - 1)))
                    for i in range(IC)
                ], reads=[r_slot[ju], r_hnf[t]], writes=[r_ups])
                em.emit("s", lambda E, kb=kb: E.activation(sil_7[kb][:], g_ps[:], AF.Silu),
                        reads=[r_gps], writes=[r_sil[kb]])
                if e is not None:
                    em.emit("v", lambda E, kb=kb, t=t, e=e: E.scalar_tensor_tensor(
                        td_7[kb][:], sil_7[kb][:], comb[:, t * 8 + e: t * 8 + e + 1], u_ps[:],
                        ALU.mult, ALU.mult),
                        reads=[r_sil[kb], r_comb, r_ups], writes=[r_td[kb]])
                else:
                    em.emit("v", lambda E, kb=kb: E.tensor_mul(td_7[kb][:], sil_7[kb][:], u_ps[:]),
                            reads=[r_sil[kb], r_ups], writes=[r_td[kb]])
            def tp_part(t):
                kb = t % 2
                emit_tpose([td_7[kb][:, mc * 128:(mc + 1) * 128] for mc in range(4)], None,
                           [r_td[kb]], [r_tdf[kb]],
                           batch_dsts=[tdf_7[kb][:].rearrange("p (i s) -> p i s", i=4)])

            def y_part(t, jd):
                kb = t % 2
                for oc in range(4):
                    yj = (t * 4 + oc) % 2
                    em.emit_group("t", [
                        (lambda E, mc=mc, oc=oc, kb=kb, jd=jd, yj=yj: E.matmul(
                            y_ps[yj], tdf_7[kb][:, mc * 128:(mc + 1) * 128],
                            slot[jd][:, mc * 2048 + oc * 512: mc * 2048 + oc * 512 + 512],
                            start=(mc == 0), stop=(mc == 3)))
                        for mc in range(4)
                    ], reads=[r_tdf[kb], r_slot[jd]], writes=[r_yps[yj]])
                    em.emit("v", lambda E, t=t, oc=oc, yj=yj: E.scalar_tensor_tensor(
                        y_sb[:, t * H + oc * 512: t * H + oc * 512 + 512], y_ps[yj], 1.0,
                        y_sb[:, t * H + oc * 512: t * H + oc * 512 + 512], ALU.mult, ALU.add),
                        reads=[r_yps[yj]], writes=[r_y])

            for e in range(NEXP):
                jg = load_slot(wg_loader(wg, e))
                ju = load_slot(wg_loader(wu, e))
                jd = load_slot(wd_loader(e))
                prev = None
                for t in range(TPP):
                    gut_part(t, e, jg, ju)
                    if prev is not None:
                        y_part(prev, jd)
                    tp_part(t)
                    prev = t
                y_part(prev, jd)

            for half in range(2):
                jg = load_slot(sw_loader(swg, half))
                ju = load_slot(sw_loader(swu, half))
                tp_pend = []
                for t in range(TPP):
                    kb = t % 2
                    g_ps, u_ps = g_ps2[kb], u_ps2[kb]
                    r_gps, r_ups = r_gps2[kb], r_ups2[kb]
                    em.emit_group("t", [
                        (lambda E, i=i, t=t, jg=jg, g_ps=g_ps: E.matmul(
                            g_ps[:], hnf[:, i * TPP * 128 + t * 128: i * TPP * 128 + (t + 1) * 128],
                            slot[jg][:, i * 512:(i + 1) * 512], start=(i == 0), stop=(i == IC - 1)))
                        for i in range(IC)
                    ], reads=[r_slot[jg], r_hnf[t]], writes=[r_gps])
                    em.emit_group("t", [
                        (lambda E, i=i, t=t, ju=ju, u_ps=u_ps: E.matmul(
                            u_ps[:], hnf[:, i * TPP * 128 + t * 128: i * TPP * 128 + (t + 1) * 128],
                            slot[ju][:, i * 512:(i + 1) * 512], start=(i == 0), stop=(i == IC - 1)))
                        for i in range(IC)
                    ], reads=[r_slot[ju], r_hnf[t]], writes=[r_ups])
                    em.emit("s", lambda E, kb=kb, g_ps=g_ps: E.activation(sil_7[kb][:], g_ps[:], AF.Silu),
                            reads=[r_gps], writes=[r_sil[kb]])
                    em.emit("v", lambda E, kb=kb, u_ps=u_ps: E.tensor_mul(td_7[kb][:], sil_7[kb][:], u_ps[:]),
                            reads=[r_sil[kb], r_ups], writes=[r_td[kb]])
                    tp_pend.append((t, kb))
                    if len(tp_pend) > 1:
                        pt_, pkb = tp_pend.pop(0)
                        emit_tpose([td_7[pkb][:, mc * 128:(mc + 1) * 128] for mc in range(4)], None,
                                   [r_td[pkb]], [r_tdfs[half]],
                                   batch_dsts=[tdfs[half][:, pt_ * 512:(pt_ + 1) * 512]
                                               .rearrange("p (i s) -> p i s", i=4)])
                while tp_pend:
                    pt_, pkb = tp_pend.pop(0)
                    emit_tpose([td_7[pkb][:, mc * 128:(mc + 1) * 128] for mc in range(4)], None,
                               [r_td[pkb]], [r_tdfs[half]],
                               batch_dsts=[tdfs[half][:, pt_ * 512:(pt_ + 1) * 512]
                                           .rearrange("p (i s) -> p i s", i=4)])
            for half in range(2):
                jd = load_slot(swd_loader(half))
                for t in range(TPP):
                    for oc in range(4):
                        yj = (t * 4 + oc) % 2
                        em.emit_group("t", [
                            (lambda E, mc=mc, oc=oc, t=t, jd=jd, half=half, yj=yj: E.matmul(
                                y_ps[yj], tdfs[half][:, t * 512 + mc * 128: t * 512 + (mc + 1) * 128],
                                slot[jd][:, mc * 2048 + oc * 512: mc * 2048 + oc * 512 + 512],
                                start=(mc == 0), stop=(mc == 3)))
                            for mc in range(4)
                        ], reads=[r_tdfs[half], r_slot[jd]], writes=[r_yps[yj]])
                        em.emit("v", lambda E, t=t, oc=oc, yj=yj: E.scalar_tensor_tensor(
                            y_sb[:, t * H + oc * 512: t * H + oc * 512 + 512], y_ps[yj], 1.0,
                            y_sb[:, t * H + oc * 512: t * H + oc * 512 + 512], ALU.mult, ALU.add),
                            reads=[r_yps[yj]], writes=[r_y])

            for t in range(TPP):
                em.dma("y", lambda E, t=t, pa=pa: E.dma_start(out=ht_7[0][:],
                                                              in_=h_s[pa * TPP + t]),
                       reads=[r_hs[pa * TPP + t]], writes=[r_ht7[0]])
                em.emit("v", lambda E, t=t: E.tensor_add(
                    y_sb[:, t * H:(t + 1) * H], y_sb[:, t * H:(t + 1) * H], ht_7[0][:]),
                    reads=[r_y, r_ht7[0]], writes=[r_yfin[t]])
                gt = pa * TPP + t
                st_toks.append(em.dma("y", lambda E, gt=gt, t=t: E.dma_start(
                    out=out[gt * 128:(gt + 1) * 128, :], in_=y_sb[:, t * H:(t + 1) * H]),
                    reads=[r_yfin[t]]))
            if pa == 0:
                em.fence()

        em.wait_tokens("y", st_toks)
        p7.close()

        block = E_(nc.Block())
        em.emit_into_blocks(block)
    return nc


def host_prep(X, W_qkv, b_qkv, W_out, b_out, gate_w, Wg, Wu, Wd, sWg, sWu, sWd):
    X = np.asarray(X, dtype=np.float32)
    B_, S_, H_ = X.shape
    c = np.ascontiguousarray
    wqkv_t = c(np.asarray(W_qkv, np.float32).T.reshape(16, 128, 48, 128)
               .transpose(2, 1, 0, 3).reshape(48, 128, 2048))
    bqkv_pc = c(np.asarray(b_qkv, np.float32).reshape(48, 128).T)
    wout_t = c(np.asarray(W_out, np.float32).T.reshape(16, 128, 2048))
    bout_row = c(np.asarray(b_out, np.float32).reshape(1, 2048))
    gate_wt = c(np.asarray(gate_w, np.float32).T.reshape(16, 128, 8))
    wg = c(np.asarray(Wg, np.float32).reshape(8, 16, 128, 512)
           .transpose(0, 2, 1, 3).reshape(8, 128, 8192))
    wu = c(np.asarray(Wu, np.float32).reshape(8, 16, 128, 512)
           .transpose(0, 2, 1, 3).reshape(8, 128, 8192))
    wd = c(np.asarray(Wd, np.float32).reshape(8, 4, 128, 2048)
           .transpose(0, 2, 1, 3).reshape(8, 128, 8192))
    _swg_r = np.asarray(sWg, np.float32).reshape(16, 128, 1024).transpose(1, 0, 2)
    swg = c(np.stack([_swg_r[:, :, hh * 512:(hh + 1) * 512].reshape(128, 8192)
                      for hh in (0, 1)]))
    _swu_r = np.asarray(sWu, np.float32).reshape(16, 128, 1024).transpose(1, 0, 2)
    swu = c(np.stack([_swu_r[:, :, hh * 512:(hh + 1) * 512].reshape(128, 8192)
                      for hh in (0, 1)]))
    swd = c(np.asarray(sWd, np.float32).reshape(2, 4, 128, 2048)
            .transpose(0, 2, 1, 3).reshape(2, 128, 8192))

    ii = np.arange(128)[:, None]
    jj = np.arange(256)[None, :]
    masks = [c(np.where(jj <= ii + 128 * p, 0.0, -1e9).astype(np.float32)) for p in (0, 1)]
    maskts = [c(np.concatenate([m.T[0:128], m.T[128:256]], axis=1)) for m in masks]
    ident = np.eye(128, dtype=np.float32)
    own = [np.concatenate([np.arange(256 * j + 128 * p, 256 * j + 128 * p + 128)
                           for j in range(S_ // 256)]) for p in (0, 1)]

    in_maps = []
    for core in range(8):
        b, p = core // 2, core % 2
        in_maps.append(dict(
            x_seq=c(X[b % B_]), x_own=c(X[b % B_][own[p]]),
            wqkv_t=wqkv_t, bqkv_pc=bqkv_pc, wout_t=wout_t, bout_row=bout_row,
            gate_wt=gate_wt, wg=wg, wu=wu, wd=wd, swg=swg, swu=swu, swd=swd,
            mask=masks[p], maskt=maskts[p], ident=ident,
        ))
    return in_maps, own


def kernel(L0_attn_x, L0_attn_W_qkv, L0_attn_b_qkv, L0_attn_W_out, L0_attn_b_out,
           L0_ffn_gate_w, L0_ffn_Wg, L0_ffn_Wu, L0_ffn_Wd, L0_ffn_sWg, L0_ffn_sWu,
           L0_ffn_sWd):
    X = np.asarray(L0_attn_x, dtype=np.float32)
    in_maps, own = host_prep(X, L0_attn_W_qkv, L0_attn_b_qkv, L0_attn_W_out,
                             L0_attn_b_out, L0_ffn_gate_w, L0_ffn_Wg, L0_ffn_Wu,
                             L0_ffn_Wd, L0_ffn_sWg, L0_ffn_sWu, L0_ffn_sWd)
    nc = build_nc(S=X.shape[1])
    res = run_bass_kernel_spmd(nc, in_maps, core_ids=list(range(8)))
    Y = np.empty_like(X)
    for core in range(8):
        b, p = core // 2, core % 2
        Y[b][own[p]] = res.results[core]["out"]
    return Y
